# version 19
# baseline (speedup 1.0000x reference)
import numpy as np
import ml_dtypes
import concourse.bass as bass
import concourse.mybir as mybir
from concourse.bass_utils import run_bass_kernel_spmd
from contextlib import ExitStack

F32 = mybir.dt.float32
BF16 = mybir.dt.bfloat16
AF = mybir.ActivationFunctionType
ALU = mybir.AluOpType
AX = mybir.AxisListType

D = 4096
S = 4096
T = 2048
NCH = 32
NH = 32
NKV = 8
HD = 128
KV = 1024
EPS = 1e-5
ENGS = ["sync", "tensor", "vector", "scalar", "gpsimd"]


class Ctx:

    def __init__(self, nc, es):
        self.nc = nc
        self.es = es
        self.h = {}
        self.n = {}
        self.base = {}
        self.pool = []
        self.keep = set()
        self.bar_n = 0

    def sem(self, name):
        if name not in self.h:
            if self.pool:
                h, b = self.pool.pop()
            else:
                h = self.nc.alloc_semaphore(name=name)
                b = 0
            self.h[name] = h
            self.base[name] = b
            self.n[name] = b
        return self.h[name]

    def release_phase(self, before):
        for name in list(self.h):
            if name not in before and name not in self.keep:
                self.pool.append((self.h.pop(name), self.n.pop(name)))
                del self.base[name]

    def inc(self, inst, name, k=1):
        inst.then_inc(self.sem(name), k)
        self.n[name] += k
        return self.n[name]

    def dma(self, eng, out, in_, name):
        eng.dma_start(out=out, in_=in_).then_inc(self.sem(name), 16)
        self.n[name] += 16

    def dma_big(self, eng, out, in_, name):
        h = out.shape[1] // 2
        eng.dma_start(out=out[:, 0:h, :], in_=in_[:, 0:h, :]).then_inc(self.sem(name), 16)
        eng.dma_start(out=out[:, h:2 * h, :], in_=in_[:, h:2 * h, :]).then_inc(self.sem(name), 16)
        self.n[name] += 32

    def wait(self, eng, name, val):
        if val > 0:
            eng.wait_ge(self.sem(name), self.base[name] + val)

    def waitn(self, eng, name):
        eng.wait_ge(self.sem(name), self.n[name])

    def vchain(self, eng, inst):
        self.inc(inst, "vv")
        eng.wait_ge(self.sem("vv"), self.n["vv"])


def build_nc(dbg=False):
    nc = bass.Bass("TRN2", target_bir_lowering=False, debug=False,
                   dynamic_dma_scratch_size=16384)
    KIND = "ExternalOutput" if dbg else "Internal"
    es = ExitStack()
    C = Ctx(nc, es)

    x_all = nc.dram_tensor("x_all", [D, S], F32, kind="ExternalInput").ap()
    wq_t = nc.dram_tensor("wq_t", [D, D], F32, kind="ExternalInput").ap()
    wk_t = nc.dram_tensor("wk_t", [D, KV], F32, kind="ExternalInput").ap()
    wv_t = nc.dram_tensor("wv_t", [D, KV], F32, kind="ExternalInput").ap()
    wo_t = nc.dram_tensor("wo_t", [D, D], F32, kind="ExternalInput").ap()
    wb_t = nc.dram_tensor("wb_t", [D, D], F32, kind="ExternalInput").ap()
    wg_col = nc.dram_tensor("wg_col", [128, NCH * 8], F32, kind="ExternalInput").ap()
    a_col = nc.dram_tensor("a_col", [128, NCH * 128], F32, kind="ExternalInput").ap()
    b_re = nc.dram_tensor("b_re", [128, D], F32, kind="ExternalInput").ap()
    qs_col_d = nc.dram_tensor("qs_col", [128, 32], F32, kind="ExternalInput").ap()
    ks_col_d = nc.dram_tensor("ks_col", [128, 8], F32, kind="ExternalInput").ap()
    vs_col_d = nc.dram_tensor("vs_col", [128, 8], F32, kind="ExternalInput").ap()
    os_col_d = nc.dram_tensor("os_col", [128, 32], F32, kind="ExternalInput").ap()
    bb_col_d = nc.dram_tensor("bb_col", [128, 32], F32, kind="ExternalInput").ap()
    ident_in = nc.dram_tensor("ident_in", [128, 128], F32, kind="ExternalInput").ap()
    mexp_in = nc.dram_tensor("mexp_in", [8, 128], F32, kind="ExternalInput").ap()

    out = nc.dram_tensor("out", [T, D], F32, kind="ExternalOutput").ap()

    xn1s = nc.dram_tensor("xn1s", [D, S], BF16, kind=KIND).ap()
    ks_d = nc.dram_tensor("ks_d", [KV, S], BF16, kind=KIND).ap()
    vs_d = nc.dram_tensor("vs_d", [S, KV], BF16, kind=KIND).ap()
    qs_d = nc.dram_tensor("qs_d", [D, T], BF16, kind=KIND).ap()
    ats_d = nc.dram_tensor("ats_d", [D, T], BF16, kind=KIND).ap()
    y1s_d = nc.dram_tensor("y1s_d", [D, T], F32, kind=KIND).ap()
    if dbg:
        dbg_call = nc.dram_tensor("dbg_call", [128, NH], F32, kind="ExternalOutput").ap()
        dbg_gq = nc.dram_tensor("dbg_gq", [128, 32], F32, kind="ExternalOutput").ap()
        dbg_gv = nc.dram_tensor("dbg_gv", [128, 8], F32, kind="ExternalOutput").ap()
        dbg_go = nc.dram_tensor("dbg_go", [128, 32], F32, kind="ExternalOutput").ap()
        dbg_hw = nc.dram_tensor("dbg_hw", [128, T], BF16, kind="ExternalOutput").ap()
        dbg_xn2 = nc.dram_tensor("dbg_xn2", [D, T], BF16, kind="ExternalOutput").ap()
        dbg_gk = nc.dram_tensor("dbg_gk", [128, 8], F32, kind="ExternalOutput").ap()
        dbg_sq = nc.dram_tensor("dbg_sq", [128, 2], F32, kind="ExternalOutput").ap()
        dbg_qsub = nc.dram_tensor("dbg_qsub", [128, NH * 128], BF16, kind="ExternalOutput").ap()
        dbg_sk = nc.dram_tensor("dbg_sk", [128, 8], F32, kind="ExternalOutput").ap()
        dbg_wk = nc.dram_tensor("dbg_wk", [128, KV], BF16, kind="ExternalOutput").ap()

    def sb(name, shape, dt, stack=None, side=None):
        return (stack or es).enter_context(nc.sbuf_tensor(name, shape, dt, side=side))

    def psum(name, shape, dt=F32, stack=None):
        return (stack or es).enter_context(nc.psum_tensor(name, shape, dt))

    blk = nc.Block().__enter__()
    C.keep = {"vv", "st_dbg"} | {"bar_" + e for e in ENGS}

    def run_phase(fn):
        before = set(C.h)
        with nc.named_scope(fn.__name__):
            fn()
            barrier()
        C.release_phase(before)

    def barrier():
        C.bar_n += 1
        cnum = C.bar_n
        for e in ENGS:
            def body(eng, e=e, cnum=cnum):
                if e != "sync":
                    eng.drain()
                C.inc(eng.nop(), "bar_" + e)
                for o in ENGS:
                    if o != e:
                        C.wait(eng, "bar_" + o, cnum)
            getattr(blk, e)(body)

    ident = sb("ident", [128, 128], F32)
    ident_bf = sb("ident_bf", [128, 128], BF16)
    mexp_bf = sb("mexp_bf", [8, 128], BF16)
    ones_col = sb("ones_col", [128, 1], BF16)
    ones_colf = sb("ones_colf", [128, 1], F32)
    ones_row = sb("ones_row", [1, 128], F32)
    sc_qs = sb("sc_qs", [128, 32], F32)
    sc_ks = sb("sc_ks", [128, 8], F32)
    sc_vs = sb("sc_vs", [128, 8], F32)
    sc_os = sb("sc_os", [128, 32], F32)
    sc_bb = sb("sc_bb", [128, 32], F32)
    gq_col = sb("gq_col", [128, 32], F32)
    gv_col = sb("gv_col", [128, 8], F32)
    go_col = sb("go_col", [128, 32], F32)
    gk_comb = sb("gk_comb", [128, 8], F32)
    c_all = sb("c_all", [128, NH], F32)
    eps1 = sb("eps1", [1, 1], F32)

    def p0_consts():
        mexp32 = sb("mexp32", [8, 128], F32)

        def f_sync(eng):
            C.dma(eng, ident[:], ident_in, "ld0")
            C.dma(eng, mexp32[:], mexp_in, "ld0")
            C.dma(eng, sc_qs[:], qs_col_d, "ld0")
            C.dma(eng, sc_ks[:], ks_col_d, "ld0")
            C.dma(eng, sc_vs[:], vs_col_d, "ld0")
            C.dma(eng, sc_os[:], os_col_d, "ld0")
            C.dma(eng, sc_bb[:], bb_col_d, "ld0")
        blk.sync(f_sync)

        def f_vec(eng):
            C.wait(eng, "ld0", 7 * 16)
            eng.tensor_copy(ident_bf[:], ident[:])
            eng.tensor_copy(mexp_bf[:], mexp32[:])
            eng.memset(ones_col[:], 1.0)
            eng.memset(ones_colf[:], 1.0)
            eng.memset(ones_row[:], 1.0)
            eng.memset(eps1[:], EPS)
        blk.vector(f_vec)
    run_phase(p0_consts)

    def prep_bitnet(w_t, o_cols, w_s, s_fin, tag):
        n_ot = o_cols // 128
        ist = ExitStack()
        ring = sb(tag + "_ring", [128, 2, o_cols], F32, ist)
        absb = sb(tag + "_abs", [128, 2, o_cols], BF16, ist)
        s_ps = [psum(tag + "_sps%d" % j, [128, 1], F32, ist) for j in range(n_ot)]
        ld, sg, ab, pq = tag + "_ld", tag + "_sg", tag + "_ab", tag + "_pq"

        def f_sync(eng):
            for c in range(NCH):
                C.wait(eng, sg, c - 1)
                C.wait(eng, ab, c - 1)
                C.dma(eng, ring[:, c % 2, :], w_t[c * 128:(c + 1) * 128, 0:o_cols], ld + str(c % 2))
        blk.sync(f_sync)

        def f_scalar(eng):
            for c in range(NCH):
                C.wait(eng, ld + str(c % 2), 16 * (c // 2 + 1))
                C.inc(eng.activation(out=w_s[:, c, :], in_=ring[:, c % 2, :], func=AF.Sign), sg)
                C.wait(eng, pq, c - 1)
                C.inc(eng.activation(out=absb[:, c % 2, :], in_=ring[:, c % 2, :], func=AF.Abs), ab)
        blk.scalar(f_scalar)

        def f_pe(eng):
            for c in range(NCH):
                C.wait(eng, ab, c + 1)
                for j in range(n_ot):
                    inst = eng.matmul(s_ps[j][:], absb[:, c % 2, j * 128:(j + 1) * 128],
                                      ones_col[:], start=(c == 0), stop=(c == NCH - 1))
                C.inc(inst, pq)
        blk.tensor(f_pe)

        def f_vec2(eng):
            C.wait(eng, pq, NCH)
            for j in range(n_ot):
                inst = eng.tensor_scalar(s_fin[:, j:j + 1], s_ps[j][:], 1.0 / D, 1e-8, ALU.mult, ALU.max)
            C.inc(inst, tag + "_done")
        blk.vector(f_vec2)
        ist.close()

    def p1a():
        st = ExitStack()
        wv_s = sb("wv_s", [128, NCH, KV], BF16, st)
        sv_fin = sb("sv_fin", [128, 8], F32, st)
        prep_bitnet(wv_t, KV, wv_s, sv_fin, "pv")

        def f_vgv(eng):
            C.wait(eng, "pv_done", 1)
            C.inc(eng.tensor_mul(gv_col[:], sv_fin[:], sc_vs[:]), "gv_done")
        blk.vector(f_vgv)
        barrier()

        SLAB = 256
        NSL = S // SLAB
        xsl = sb("xsl", [128, NCH, SLAB], F32, st)
        xsq = sb("xsq", [128, NCH, SLAB], BF16, st)
        xn_sl = sb("xn_sl", [128, 2, NCH, SLAB], BF16, st)
        m_row = sb("m_row", [1, SLAB], F32, st)
        v_row = sb("v_row", [1, SLAB], F32, st)
        sd_row = sb("sd_row", [1, SLAB], F32, st)
        r_row = sb("r_row", [1, SLAB], F32, st)
        m_b = sb("m_b", [128, SLAB], F32, st)
        r_b = sb("r_b", [128, SLAB], F32, st)
        tmp_a = sb("tmp_a", [128, SLAB], F32, st)
        vev = sb("vev", [128, 2, KV], BF16, st)
        p_sum = psum("p_sum", [1, SLAB], F32, st)
        p_sq = psum("p_sq", [1, SLAB], F32, st)
        p_mb = psum("p_mb", [128, SLAB], F32, st)
        p_rb = psum("p_rb", [128, SLAB], F32, st)
        p_v = [psum("p_v%d" % i, [128, 512], F32, st) for i in range(4)]
        x_re = x_all.rearrange("(c p) t -> p c t", p=128)
        xn_re = xn1s.rearrange("(c p) t -> p c t", p=128)

        def f_sync(eng):
            for sl in range(NSL):
                C.wait(eng, "ve_ap", sl)
                C.dma_big(eng, xsl[:], x_re[:, :, sl * SLAB:(sl + 1) * SLAB], "ld_x")
        blk.sync(f_sync)

        def f_scalar(eng):
            for sl in range(NSL):
                C.wait(eng, "ld_x", 32 * (sl + 1))
                C.wait(eng, "pe_st", sl)
                for c in range(NCH):
                    inst = eng.activation(out=xsq[:, c, :], in_=xsl[:, c, :], func=AF.Square)
                C.inc(inst, "sc_cast")
                C.wait(eng, "ve_var", sl + 1)
                C.inc(eng.activation(out=sd_row[:], in_=v_row[:], func=AF.Sqrt, bias=eps1[:]), "sc_sd")
        blk.scalar(f_scalar)

        def f_pe(eng):
            C.wait(eng, "gv_done", 1)
            for sl in range(NSL):
                C.wait(eng, "sc_cast", sl + 1)
                for c in range(NCH):
                    eng.matmul(p_sum[:], ones_colf[:], xsl[:, c, :], start=(c == 0), stop=(c == NCH - 1))
                    inst = eng.matmul(p_sq[:], ones_col[:], xsq[:, c, :], start=(c == 0), stop=(c == NCH - 1))
                C.inc(inst, "pe_st")
                C.wait(eng, "ve_mr", sl + 1)
                eng.matmul(p_mb[:], ones_row[:], m_row[:], start=True, stop=True)
                inst = eng.matmul(p_rb[:], ones_row[:], r_row[:], start=True, stop=True)
                C.inc(inst, "pe_bc")
                C.wait(eng, "ve_ap", sl + 1)
                for tt in range(2):
                    for hf in range(2):
                        pvt = p_v[tt * 2 + hf]
                        for c in range(NCH):
                            inst = eng.matmul(pvt[:], xn_sl[:, sl % 2, c, tt * 128:(tt + 1) * 128],
                                              wv_s[:, c, hf * 512:(hf + 1) * 512],
                                              start=(c == 0), stop=(c == NCH - 1))
                        C.inc(inst, "pe_v")
        blk.tensor(f_pe)

        def f_vec(eng):
            for sl in range(NSL):
                C.wait(eng, "pe_st", sl + 1)
                eng.tensor_scalar_mul(m_row[:], p_sum[:], 1.0 / D)
                C.vchain(eng, eng.tensor_scalar_mul(v_row[:], p_sq[:], 1.0 / D))
                C.vchain(eng, eng.tensor_mul(tmp_a[0:1, :], m_row[:], m_row[:]))
                C.inc(eng.tensor_sub(v_row[:], v_row[:], tmp_a[0:1, :]), "ve_var")
                C.wait(eng, "sc_sd", sl + 1)
                C.inc(eng.reciprocal(r_row[:], sd_row[:]), "ve_mr")
                C.wait(eng, "pe_bc", sl + 1)
                eng.tensor_copy(m_b[:], p_mb[:])
                C.vchain(eng, eng.tensor_copy(r_b[:], p_rb[:]))
                C.wait(eng, "st_xn", 32 * (sl - 1))
                C.wait(eng, "pe_v", 4 * sl)
                for c in range(NCH):
                    C.vchain(eng, eng.tensor_sub(tmp_a[:], xsl[:, c, :], m_b[:]))
                    inst = eng.tensor_mul(xn_sl[:, sl % 2, c, :], tmp_a[:], r_b[:])
                C.inc(inst, "ve_ap")
                for tt in range(2):
                    C.wait(eng, "st_v%d" % tt, 16 * ((sl * 2 + tt) // 2))
                    for hf in range(2):
                        C.wait(eng, "pe_v", sl * 4 + tt * 2 + hf + 1)
                        inst = eng.tensor_copy(vev[:, tt, hf * 512:(hf + 1) * 512], p_v[tt * 2 + hf][:])
                    C.inc(inst, "ve_vev")
        blk.vector(f_vec)

        def f_g(eng):
            for sl in range(NSL):
                C.wait(eng, "ve_ap", sl + 1)
                C.dma_big(eng, xn_re[:, :, sl * SLAB:(sl + 1) * SLAB], xn_sl[:, sl % 2, :, :], "st_xn")
                for tt in range(2):
                    C.wait(eng, "ve_vev", sl * 2 + tt + 1)
                    row = sl * SLAB + tt * 128
                    C.dma(eng, vs_d[row:row + 128, :], vev[:, tt, :], "st_v%d" % tt)
            C.wait(eng, "st_xn", 32 * NSL)
            C.wait(eng, "st_v0", 16 * NSL)
            C.wait(eng, "st_v1", 16 * NSL)
        blk.gpsimd(f_g)
        st.close()
    run_phase(p1a)

    ksub_stack = ExitStack()
    K_sub = sb("K_sub", [128, NKV * 512], BF16, ksub_stack)
    Q_sub = sb("Q_sub", [128, NH * 128], BF16, ksub_stack)

    def p1b():
        st = ExitStack()
        wk_s = sb("wk_s", [128, NCH, KV], BF16, st)
        sk_fin = sb("sk_fin", [128, 8], F32, st)
        prep_bitnet(wk_t, KV, wk_s, sk_fin, "pk")

        def f_vgk(eng):
            C.wait(eng, "pk_done", 1)
            C.inc(eng.scalar_tensor_tensor(gk_comb[:], sk_fin[:], 1.0 / np.sqrt(HD), sc_ks[:],
                                           ALU.mult, ALU.mult), "gk_done")
        blk.vector(f_vgk)
        barrier()

        SLAB = 512
        NSL = S // SLAB
        xnsl = sb("xnsl", [128, 2, NCH, SLAB], BF16, st)
        kev = sb("kev", [128, 2, SLAB], BF16, st)
        p_k = [psum("p_k%d" % i, [128, SLAB], F32, st) for i in range(2)]
        xn_re = xn1s.rearrange("(c p) t -> p c t", p=128)

        def f_sync(eng):
            for sl in range(NSL):
                C.wait(eng, "pe_k", 8 * (sl - 1))
                C.dma_big(eng, xnsl[:, sl % 2, :, :], xn_re[:, :, sl * SLAB:(sl + 1) * SLAB], "ld_xn%d" % (sl % 2))
        blk.sync(f_sync)

        def f_pe(eng):
            C.wait(eng, "gk_done", 1)
            for sl in range(NSL):
                C.wait(eng, "ld_xn%d" % (sl % 2), 32 * (sl // 2 + 1))
                for j in range(8):
                    C.wait(eng, "ve_k", sl * 8 + j - 1)
                    for c in range(NCH):
                        inst = eng.matmul(p_k[j % 2][:], wk_s[:, c, j * 128:(j + 1) * 128],
                                          xnsl[:, sl % 2, c, :], start=(c == 0), stop=(c == NCH - 1))
                    C.inc(inst, "pe_k")
        blk.tensor(f_pe)

        def f_vec(eng):
            for sl in range(NSL):
                for j in range(8):
                    C.wait(eng, "pe_k", sl * 8 + j + 1)
                    C.wait(eng, "st_k%d" % (j % 2), 16 * ((sl * 8 + j) // 2))
                    C.vchain(eng, eng.tensor_copy(kev[:, j % 2, :], p_k[j % 2][:]))
                    C.inc(eng.tensor_copy(K_sub[:, j * 512 + sl * 64: j * 512 + (sl + 1) * 64],
                                          kev[:, j % 2, ::8]), "ve_k")
        blk.vector(f_vec)

        def f_g(eng):
            for sl in range(NSL):
                for j in range(8):
                    C.wait(eng, "ve_k", sl * 8 + j + 1)
                    C.dma(eng, ks_d[j * 128:(j + 1) * 128, sl * SLAB:(sl + 1) * SLAB],
                          kev[:, j % 2, :], "st_k%d" % (j % 2))
            C.wait(eng, "st_k0", 16 * NSL * 4)
            C.wait(eng, "st_k1", 16 * NSL * 4)
            if dbg:
                C.dma(eng, dbg_sk, sk_fin[:], "st_dbg")
                C.dma(eng, dbg_wk, wk_s[:, 0, :], "st_dbg")
                eng.wait_ge(C.sem("st_dbg"), C.n["st_dbg"])
        blk.gpsimd(f_g)
        st.close()
    run_phase(p1b)

    def p2():
        st = ExitStack()
        xn_own = sb("xn_own", [128, NCH, T], BF16, st)
        wq_ring = sb("wq_ring", [128, 4, 256], F32, st)
        wq_s = sb("wq_s", [128, NCH, 256], BF16, st)
        absb = sb("p2_abs", [128, 2, 256], BF16, st)
        sq_fin = sb("sq_fin", [128, 2], F32, st)
        qev = sb("qev", [128, 2, 512], BF16, st)
        p_sq2 = [psum("p_sq2_%d" % j, [128, 1], F32, st) for j in range(2)]
        p_q = [psum("p_q%d" % i, [128, 512], F32, st) for i in range(2)]
        xn_re = xn1s.rearrange("(c p) t -> p c t", p=128)

        def f_sync(eng):
            for i in range(4):
                C.dma_big(eng, xn_own[:, :, i * 512:(i + 1) * 512],
                          xn_re[:, :, i * 512:(i + 1) * 512], "ld_xo")
            for hs in range(16):
                for c in range(NCH):
                    k = hs * NCH + c
                    C.wait(eng, "sc_q", k - 3)
                    C.wait(eng, "ve_qab", k - 3)
                    C.dma(eng, wq_ring[:, k % 4, :],
                          wq_t[c * 128:(c + 1) * 128, hs * 256:(hs + 1) * 256], "ld_wq%d" % (k % 4))
        blk.sync(f_sync)

        def f_scalar(eng):
            for hs in range(16):
                for c in range(NCH):
                    k = hs * NCH + c
                    C.wait(eng, "ld_wq%d" % (k % 4), 16 * (k // 4 + 1))
                    if c == 0:
                        C.wait(eng, "pe_q", hs * 8)
                    C.inc(eng.activation(out=wq_s[:, c, :], in_=wq_ring[:, k % 4, :], func=AF.Sign), "sc_q")
                    C.wait(eng, "pe_sq2", k - 1)
                    C.inc(eng.activation(out=absb[:, k % 2, :], in_=wq_ring[:, k % 4, :], func=AF.Abs), "ve_qab")
        blk.scalar(f_scalar)

        def f_pe(eng):
            for hs in range(16):
                for c in range(NCH):
                    k = hs * NCH + c
                    C.wait(eng, "ve_qab", k + 1)
                    for j in range(2):
                        inst = eng.matmul(p_sq2[j][:], absb[:, k % 2, j * 128:(j + 1) * 128],
                                          ones_col[:], start=(c == 0), stop=(c == NCH - 1))
                    C.inc(inst, "pe_sq2")
                C.wait(eng, "sc_q", (hs + 1) * NCH)
                C.wait(eng, "ve_gq", hs + 1)
                for j in range(2):
                    for ts_ in range(4):
                        u = j * 4 + ts_
                        if hs == 0 and j == 0:
                            C.wait(eng, "ld_xo", 32 * (ts_ + 1))
                        C.wait(eng, "ve_q", hs * 8 + u - 1)
                        for c in range(NCH):
                            inst = eng.matmul(p_q[u % 2][:], wq_s[:, c, j * 128:(j + 1) * 128],
                                              xn_own[:, c, ts_ * 512:(ts_ + 1) * 512],
                                              start=(c == 0), stop=(c == NCH - 1))
                        C.inc(inst, "pe_q")
        blk.tensor(f_pe)

        def f_vec(eng):
            for hs in range(16):
                C.wait(eng, "pe_sq2", (hs + 1) * NCH)
                for j in range(2):
                    inst = eng.tensor_scalar(sq_fin[:, j:j + 1], p_sq2[j][:], 1.0 / D, 1e-8, ALU.mult, ALU.max)
                C.vchain(eng, inst)
                for j in range(2):
                    ot = hs * 2 + j
                    inst = eng.scalar_tensor_tensor(gq_col[:, ot:ot + 1], sq_fin[:, j:j + 1],
                                                    sc_qs[:, ot:ot + 1], gk_comb[:, ot // 4:ot // 4 + 1],
                                                    ALU.mult, ALU.mult)
                C.inc(inst, "ve_gq")
                C.waitn(eng, "ve_gq")
                for j in range(2):
                    for ts_ in range(4):
                        u = j * 4 + ts_
                        ot = hs * 2 + j
                        C.wait(eng, "pe_q", hs * 8 + u + 1)
                        C.wait(eng, "st_q%d" % (u % 2), 16 * ((hs * 8 + u) // 2))
                        C.vchain(eng, eng.tensor_scalar_mul(qev[:, u % 2, :], p_q[u % 2][:], gq_col[:, ot:ot + 1]))
                        C.inc(eng.tensor_copy(Q_sub[:, ot * 128 + ts_ * 32: ot * 128 + (ts_ + 1) * 32],
                                              qev[:, u % 2, ::16]), "ve_q")
        blk.vector(f_vec)

        def f_g(eng):
            for hs in range(16):
                for j in range(2):
                    for ts_ in range(4):
                        u = j * 4 + ts_
                        ot = hs * 2 + j
                        C.wait(eng, "ve_q", hs * 8 + u + 1)
                        C.dma(eng, qs_d[ot * 128:(ot + 1) * 128, ts_ * 512:(ts_ + 1) * 512],
                              qev[:, u % 2, :], "st_q%d" % (u % 2))
            C.wait(eng, "st_q0", 16 * 64)
            C.wait(eng, "st_q1", 16 * 64)
            if dbg:
                C.dma(eng, dbg_gk, gk_comb[:], "st_dbg")
                C.dma(eng, dbg_sq, sq_fin[:], "st_dbg")
                C.dma(eng, dbg_qsub, Q_sub[:], "st_dbg")
                eng.wait_ge(C.sem("st_dbg"), C.n["st_dbg"])
        blk.gpsimd(f_g)
        st.close()
    run_phase(p2)

    def pc():
        st = ExitStack()
        cmax = sb("cmax", [128, NH], F32, st)
        tcm = sb("tcm", [32, 128], F32, st)
        negc_row = sb("negc_row", [1, 32], F32, st)
        p_cs = [psum("p_cs%d" % i, [128, 512], F32, st) for i in range(2)]
        p_t1 = psum("p_t1", [32, 128], F32, st)
        p_t2 = psum("p_t2", [1, 32], F32, st)
        p_ca = psum("p_ca", [128, 32], F32, st)

        def f_pe(eng):
            for h in range(NH):
                C.wait(eng, "ve_cm", h - 1)
                C.inc(eng.matmul(p_cs[h % 2][:], Q_sub[:, h * 128:(h + 1) * 128],
                                 K_sub[:, (h // 4) * 512:(h // 4 + 1) * 512],
                                 start=True, stop=True), "pe_cs")
            C.wait(eng, "ve_cm", NH)
            C.inc(eng.transpose(p_t1[:], cmax[:], ident[:]), "pe_cs")
            C.wait(eng, "ve_cm", NH + 1)
            C.inc(eng.transpose(p_t2[:], tcm[:, 0:1], ident[0:32, 0:32]), "pe_cs")
            C.wait(eng, "ve_cm", NH + 2)
            C.inc(eng.matmul(p_ca[:], ones_row[:], negc_row[:], start=True, stop=True), "pe_cs")
        blk.tensor(f_pe)

        def f_vec(eng):
            for h in range(NH):
                C.wait(eng, "pe_cs", h + 1)
                C.inc(eng.reduce_max(cmax[:, h:h + 1], p_cs[h % 2][:], axis=AX.X), "ve_cm")
            C.wait(eng, "pe_cs", NH + 1)
            C.inc(eng.reduce_max(tcm[:, 0:1], p_t1[:], axis=AX.X, negate=True), "ve_cm")
            C.wait(eng, "pe_cs", NH + 2)
            C.inc(eng.tensor_copy(negc_row[:], p_t2[:]), "ve_cm")
            C.wait(eng, "pe_cs", NH + 3)
            C.inc(eng.tensor_copy(c_all[:], p_ca[:]), "ve_ca")
        blk.vector(f_vec)
        st.close()
    run_phase(pc)
    if dbg:
        def f_dbg1(eng):
            C.dma(eng, dbg_call, c_all[:], "st_dbg")
            C.dma(eng, dbg_gq, gq_col[:], "st_dbg")
            C.dma(eng, dbg_gv, gv_col[:], "st_dbg")
            eng.wait_ge(C.sem("st_dbg"), C.n["st_dbg"])
        blk.gpsimd(f_dbg1)
        barrier()
    ksub_stack.close()

    def p3():
        st = ExitStack()
        kbuf = sb("kbuf", [128, 2, S], BF16, st)
        vbuf = sb("vbuf", [128, 2, NCH, 128], BF16, st)
        qbuf = sb("qbuf", [128, 2, T], BF16, st)
        pt = sb("ptile", [128, 6, 512], BF16, st)
        acc3 = sb("acc3", [128, 2, 512], F32, st)
        lnrow = sb("lnrow", [1, 512], F32, st)
        rec_row = sb("rec_row", [1, 512], F32, st)
        rec_b = sb("rec_b", [128, 512], F32, st)
        atile = sb("atile", [128, 2, 512], BF16, st)
        p_sc = [psum("p3_sc%d" % i, [128, 2, 512], F32, st) for i in range(3)]
        p_pv = psum("p3_pv", [128, 512], F32, st)
        p_mix = psum("p3_mix", [128, 512], F32, st)
        vs_re = vs_d.rearrange("(kt p) d -> p kt d", p=128)

        NSP = 4
        NUPK = 4 * NSP
        NUNITS = NKV * NUPK
        NPAIR = 16

        def f_sync(eng):
            for kh in range(NKV):
                C.wait(eng, "se_v", NPAIR * NUPK * (kh - 1))
                C.dma(eng, kbuf[:, kh % 2, :], ks_d[kh * 128:(kh + 1) * 128, :], "ld_k3%d" % (kh % 2))
                C.dma_big(eng, vbuf[:, kh % 2, :, :], vs_re[:, :, kh * 128:(kh + 1) * 128], "ld_v3%d" % (kh % 2))
                for qi in range(4):
                    g = kh * 4 + qi
                    C.wait(eng, "se_v", NPAIR * NSP * (g - 1))
                    C.dma(eng, qbuf[:, g % 2, :], qs_d[(kh * 4 + qi) * 128:(kh * 4 + qi + 1) * 128, :], "ld_q3%d" % (g % 2))
        blk.sync(f_sync)

        def spair(eng, p, kb, qb):
            ps = p_sc[p % 3]
            eng.matmul(ps[:, 0, :], kb[:, (2 * p) * 128:(2 * p + 1) * 128], qb, start=True, stop=True)
            inst = eng.matmul(ps[:, 1, :], kb[:, (2 * p + 1) * 128:(2 * p + 2) * 128], qb, start=True, stop=True)
            C.inc(inst, "se_s")

        def f_pe(eng):
            u = 0
            for kh in range(NKV):
                for qi in range(4):
                    g = kh * 4 + qi
                    for sp in range(NSP):
                        base = u * NPAIR
                        if qi == 0 and sp == 0:
                            C.wait(eng, "ld_k3%d" % (kh % 2), 16 * (kh // 2 + 1))
                            C.wait(eng, "ld_v3%d" % (kh % 2), 32 * (kh // 2 + 1))
                        if sp == 0:
                            C.wait(eng, "ld_q3%d" % (g % 2), 16 * (g // 2 + 1))
                        C.wait(eng, "se_pv", base)
                        kb = kbuf[:, kh % 2, :]
                        qb = qbuf[:, g % 2, sp * 512:(sp + 1) * 512]
                        spair(eng, 0, kb, qb)
                        spair(eng, 1, kb, qb)
                        spair(eng, 2, kb, qb)
                        C.wait(eng, "ve_fin", u)
                        for p in range(NPAIR):
                            C.wait(eng, "se_pv", base + p + 1)
                            if p + 3 < NPAIR:
                                spair(eng, p + 3, kb, qb)
                            ptp = pt[:, (2 * p) % 6:(2 * p) % 6 + 2, :]
                            for i in range(2):
                                kt = 2 * p + i
                                inst = eng.matmul(p_pv[:], vbuf[:, kh % 2, kt, :], ptp[:, i, :],
                                                  start=(kt == 0), stop=(kt == 31))
                            C.inc(inst, "se_v")
                        C.wait(eng, "ve_add", 16 * (u + 1))
                        C.inc(eng.matmul(p_mix[0:1, :], ones_colf[:], acc3[:, u % 2, :],
                                         start=True, stop=True), "pe_rs")
                        C.wait(eng, "sc_rec", u + 1)
                        C.inc(eng.matmul(p_mix[:], ones_row[:], rec_row[:], start=True, stop=True), "pe_bc3")
                        u += 1
        blk.tensor(f_pe)

        def f_scalar(eng):
            for u in range(NUNITS):
                h = u // NSP
                base = u * NPAIR
                for p in range(NPAIR):
                    C.wait(eng, "se_s", base + p + 1)
                    C.wait(eng, "se_v", base + p - 2)
                    C.wait(eng, "ve_add", base + p - 2)
                    inst = eng.activation(out=pt[:, (2 * p) % 6:(2 * p) % 6 + 2, :],
                                          in_=p_sc[p % 3][:],
                                          func=AF.Exp, bias=c_all[:, h:h + 1], scale=1.0)
                    C.inc(inst, "se_pv")
                C.wait(eng, "pe_rs", u + 1)
                inst = eng.activation(out=lnrow[:], in_=p_mix[0:1, :], func=AF.Ln)
                C.inc(inst, "sc_ch")
                eng.wait_ge(C.sem("sc_ch"), C.n["sc_ch"])
                C.inc(eng.activation(out=rec_row[:], in_=lnrow[:], func=AF.Exp, scale=-1.0), "sc_rec")
        blk.scalar(f_scalar)

        def f_vec(eng):
            for u in range(NUNITS):
                base = u * NPAIR
                for p in range(NPAIR):
                    C.wait(eng, "se_pv", base + p + 1)
                    ptp = pt[:, (2 * p) % 6:(2 * p) % 6 + 2, :]
                    av = acc3[:, u % 2, :]
                    if p == 0:
                        C.wait(eng, "pe_rs", u - 1)
                        eng.tensor_add(av, ptp[:, 0, :], ptp[:, 1, :])
                    else:
                        eng.tensor_add(av, av, ptp[:, 0, :])
                        eng.tensor_add(av, av, ptp[:, 1, :])
                    C.inc(eng.nop(), "ve_add")
                C.wait(eng, "pe_bc3", u + 1)
                C.vchain(eng, eng.tensor_copy(rec_b[:], p_mix[:]))
                kh = u // NUPK
                C.wait(eng, "st_at%d" % (u % 2), 16 * (u // 2))
                C.inc(eng.scalar_tensor_tensor(atile[:, u % 2, :], p_pv[:], gv_col[:, kh:kh + 1],
                                               rec_b[:], ALU.mult, ALU.mult), "ve_fin")
        blk.vector(f_vec)

        def f_g(eng):
            for u in range(NUNITS):
                h = u // NSP
                sp = u % NSP
                C.wait(eng, "ve_fin", u + 1)
                C.dma(eng, ats_d[h * 128:(h + 1) * 128, sp * 512:(sp + 1) * 512],
                      atile[:, u % 2, :], "st_at%d" % (u % 2))
            C.wait(eng, "st_at0", 16 * NUNITS // 2)
            C.wait(eng, "st_at1", 16 * NUNITS // 2)
        blk.gpsimd(f_g)
        st.close()
    run_phase(p3)

    ln2_stack = ExitStack()
    m2_b = sb("m2_b", [128, T], F32, ln2_stack, side="right")
    r2_b = sb("r2_b", [128, T], F32, ln2_stack, side="right")

    def p4():
        st = ExitStack()
        at_own = sb("at_own", [128, NCH, T], BF16, st)
        wo_ring = sb("wo_ring", [128, 4, 256], F32, st)
        wo_s = sb("wo_s", [128, NCH, 256], BF16, st)
        absb = sb("p4_abs", [128, 2, 256], BF16, st)
        so_fin = sb("so_fin", [128, 2], F32, st)
        xt = sb("xt", [128, 2, 512], F32, st)
        y1t = sb("y1t", [128, 2, 512], F32, st)
        y1bf = sb("y1bf", [128, 2, 512], BF16, st)
        y1sq = sb("y1sq", [128, 2, 512], BF16, st)
        acc_s = sb("acc_s", [1, T], F32, st)
        acc_q = sb("acc_q", [1, T], F32, st)
        srow = sb("srow", [1, T], F32, st)
        p_so = [psum("p_so_%d" % j, [128, 1], F32, st) for j in range(2)]
        p_o = [psum("p_o%d" % i, [128, 512], F32, st) for i in range(2)]
        p_s4 = psum("p_s4", [1, 512], F32, st)
        p_q4 = psum("p_q4", [1, 512], F32, st)
        ats_re = ats_d.rearrange("(c p) t -> p c t", p=128)

        def f_sync(eng):
            for i in range(4):
                C.dma_big(eng, at_own[:, :, i * 512:(i + 1) * 512],
                          ats_re[:, :, i * 512:(i + 1) * 512], "ld_at")
            for hs in range(16):
                for c in range(NCH):
                    k = hs * NCH + c
                    C.wait(eng, "sc_o", k - 3)
                    C.wait(eng, "ve_oab", k - 3)
                    C.dma(eng, wo_ring[:, k % 4, :],
                          wo_t[c * 128:(c + 1) * 128, hs * 256:(hs + 1) * 256], "ld_wo%d" % (k % 4))
                for j in range(2):
                    for ts_ in range(4):
                        u = hs * 8 + j * 4 + ts_
                        ot = hs * 2 + j
                        C.wait(eng, "ve_y1", u - 1)
                        C.dma(eng, xt[:, u % 2, :],
                              x_all[ot * 128:(ot + 1) * 128, ts_ * 512:(ts_ + 1) * 512], "ld_xt%d" % (u % 2))
        blk.sync(f_sync)

        def f_scalar(eng):
            for hs in range(16):
                for c in range(NCH):
                    k = hs * NCH + c
                    C.wait(eng, "ld_wo%d" % (k % 4), 16 * (k // 4 + 1))
                    if c == 0:
                        C.wait(eng, "pe_o", hs * 8)
                    C.inc(eng.activation(out=wo_s[:, c, :], in_=wo_ring[:, k % 4, :], func=AF.Sign), "sc_o")
                    C.wait(eng, "pe_so", k - 1)
                    C.inc(eng.activation(out=absb[:, k % 2, :], in_=wo_ring[:, k % 4, :], func=AF.Abs), "ve_oab")
                for ui in range(8):
                    u = hs * 8 + ui
                    C.wait(eng, "ve_y1", u + 1)
                    C.wait(eng, "pe_s4", u - 1)
                    eng.activation(out=y1bf[:, u % 2, :], in_=y1t[:, u % 2, :], func=AF.Copy)
                    C.inc(eng.activation(out=y1sq[:, u % 2, :], in_=y1t[:, u % 2, :], func=AF.Square), "sc_y1")
        blk.scalar(f_scalar)

        def stats_mm(eng, u):
            C.wait(eng, "sc_y1", u + 1)
            C.wait(eng, "ve_s4", u)
            eng.matmul(p_s4[:], ones_col[:], y1bf[:, u % 2, :], start=True, stop=True)
            inst = eng.matmul(p_q4[:], ones_col[:], y1sq[:, u % 2, :], start=True, stop=True)
            C.inc(inst, "pe_s4")

        def f_pe(eng):
            for hs in range(16):
                for c in range(NCH):
                    k = hs * NCH + c
                    C.wait(eng, "ve_oab", k + 1)
                    for j in range(2):
                        inst = eng.matmul(p_so[j][:], absb[:, k % 2, j * 128:(j + 1) * 128],
                                          ones_col[:], start=(c == 0), stop=(c == NCH - 1))
                    C.inc(inst, "pe_so")
                C.wait(eng, "sc_o", (hs + 1) * NCH)
                C.wait(eng, "ve_go", hs + 1)
                for j in range(2):
                    for ts_ in range(4):
                        u = hs * 8 + j * 4 + ts_
                        if hs == 0 and j == 0:
                            C.wait(eng, "ld_at", 32 * (ts_ + 1))
                        C.wait(eng, "ve_y1", u - 1)
                        for c in range(NCH):
                            inst = eng.matmul(p_o[u % 2][:], wo_s[:, c, j * 128:(j + 1) * 128],
                                              at_own[:, c, ts_ * 512:(ts_ + 1) * 512],
                                              start=(c == 0), stop=(c == NCH - 1))
                        C.inc(inst, "pe_o")
                        if u >= 1:
                            stats_mm(eng, u - 1)
            stats_mm(eng, 127)
        blk.tensor(f_pe)

        def f_vec(eng):
            eng.memset(acc_s[:], 0.0)
            eng.memset(acc_q[:], 0.0)
            for hs in range(16):
                C.wait(eng, "pe_so", (hs + 1) * NCH)
                for j in range(2):
                    inst = eng.tensor_scalar(so_fin[:, j:j + 1], p_so[j][:], 1.0 / D, 1e-8, ALU.mult, ALU.max)
                C.vchain(eng, inst)
                for j in range(2):
                    ot = hs * 2 + j
                    inst = eng.tensor_mul(go_col[:, ot:ot + 1], so_fin[:, j:j + 1], sc_os[:, ot:ot + 1])
                C.inc(inst, "ve_go")
                C.waitn(eng, "ve_go")
                for j in range(2):
                    for ts_ in range(4):
                        u = hs * 8 + j * 4 + ts_
                        ot = hs * 2 + j
                        C.wait(eng, "pe_o", u + 1)
                        C.wait(eng, "ld_xt%d" % (u % 2), 16 * (u // 2 + 1))
                        C.wait(eng, "st_y1%d" % (u % 2), 16 * (u // 2))
                        C.wait(eng, "sc_y1", u - 1)
                        C.inc(eng.scalar_tensor_tensor(y1t[:, u % 2, :], p_o[u % 2][:],
                                                       go_col[:, ot:ot + 1], xt[:, u % 2, :],
                                                       ALU.mult, ALU.add), "ve_y1")
                        C.wait(eng, "pe_s4", u)
                        if u >= 1:
                            up = u - 1
                            tsp = up % 4
                            eng.tensor_add(acc_s[:, tsp * 512:(tsp + 1) * 512],
                                           acc_s[:, tsp * 512:(tsp + 1) * 512], p_s4[:])
                            C.inc(eng.tensor_add(acc_q[:, tsp * 512:(tsp + 1) * 512],
                                                 acc_q[:, tsp * 512:(tsp + 1) * 512], p_q4[:]), "ve_s4")
            C.wait(eng, "pe_s4", 128)
            tsp = 127 % 4
            eng.tensor_add(acc_s[:, tsp * 512:(tsp + 1) * 512],
                           acc_s[:, tsp * 512:(tsp + 1) * 512], p_s4[:])
            C.inc(eng.tensor_add(acc_q[:, tsp * 512:(tsp + 1) * 512],
                                 acc_q[:, tsp * 512:(tsp + 1) * 512], p_q4[:]), "ve_s4")
            C.vchain(eng, eng.tensor_scalar_mul(acc_s[:], acc_s[:], 1.0 / D))
            C.vchain(eng, eng.tensor_scalar_mul(acc_q[:], acc_q[:], 1.0 / D))
            C.vchain(eng, eng.tensor_mul(srow[:], acc_s[:], acc_s[:]))
            C.inc(eng.tensor_sub(acc_q[:], acc_q[:], srow[:]), "ve_v2")
        blk.vector(f_vec)

        def f_g(eng):
            for u in range(128):
                ot = (u // 8) * 2 + (u % 8) // 4
                ts_ = u % 4
                C.wait(eng, "ve_y1", u + 1)
                C.dma(eng, y1s_d[ot * 128:(ot + 1) * 128, ts_ * 512:(ts_ + 1) * 512],
                      y1t[:, u % 2, :], "st_y1%d" % (u % 2))
            C.wait(eng, "st_y10", 16 * 64)
            C.wait(eng, "st_y11", 16 * 64)
        blk.gpsimd(f_g)

        def f_sfin(eng):
            C.wait(eng, "ve_v2", 1)
            C.inc(eng.activation(out=srow[:], in_=acc_q[:], func=AF.Sqrt, bias=eps1[:]), "sc_sd2")
        blk.scalar(f_sfin)

        def f_vfin2(eng):
            C.wait(eng, "sc_sd2", 1)
            C.inc(eng.reciprocal(srow[:], srow[:]), "ve_r2")
        blk.vector(f_vfin2)

        def f_pfin(eng):
            C.wait(eng, "ve_r2", 1)
            for i in range(4):
                C.wait(eng, "ve_bc2", i)
                eng.matmul(p_o[0][:], ones_row[:], acc_s[:, i * 512:(i + 1) * 512], start=True, stop=True)
                inst = eng.matmul(p_o[1][:], ones_row[:], srow[:, i * 512:(i + 1) * 512], start=True, stop=True)
                C.inc(inst, "pe_bc2")
        blk.tensor(f_pfin)

        def f_vfin3(eng):
            for i in range(4):
                C.wait(eng, "pe_bc2", i + 1)
                eng.tensor_copy(m2_b[:, i * 512:(i + 1) * 512], p_o[0][:])
                C.inc(eng.tensor_copy(r2_b[:, i * 512:(i + 1) * 512], p_o[1][:]), "ve_bc2")
        blk.vector(f_vfin3)
        st.close()
    run_phase(p4)

    p67_stack = ExitStack()
    xn2 = sb("xn2", [128, NCH, T], BF16, p67_stack)
    hw_bf = sb("hw_bf", [128, T], BF16, p67_stack)
    a_bf = sb("a_bf", [128, NCH, 128], BF16, p67_stack)
    b_bf = sb("b_bf", [128, D], BF16, p67_stack)
    wg_bf = sb("wg_bf", [128, NCH, 8], BF16, p67_stack)

    def p5():
        st = ExitStack()
        y1c = sb("y1c", [128, 2, T], F32, st)
        tmp5 = sb("tmp5", [128, T], F32, st)

        def f_g(eng):
            for c in range(NCH):
                C.wait(eng, "ve_x2", c - 1)
                C.dma(eng, y1c[:, c % 2, :], y1s_d[c * 128:(c + 1) * 128, :], "ld_y1%d" % (c % 2))
        blk.gpsimd(f_g)

        def f_vec(eng):
            for c in range(NCH):
                C.wait(eng, "ld_y1%d" % (c % 2), 16 * (c // 2 + 1))
                C.vchain(eng, eng.tensor_sub(tmp5[:], y1c[:, c % 2, :], m2_b[:]))
                C.inc(eng.tensor_mul(xn2[:, c, :], tmp5[:], r2_b[:]), "ve_x2")
        blk.vector(f_vec)
        st.close()
    run_phase(p5)
    ln2_stack.close()

    def p6():
        st = ExitStack()
        ab32 = sb("ab32", [128, 2048], F32, st)
        lg = sb("lg", [128, 8], F32, st)
        nm1 = sb("nm1", [128, 1], F32, st)
        m1 = sb("m1", [128, 1], F32, st)
        m2r = sb("m2r", [128, 1], F32, st)
        mask1 = sb("mask1", [128, 8], F32, st)
        lmask = sb("lmask", [128, 8], F32, st)
        expt = sb("expt", [128, 8], F32, st)
        rwun = sb("rwun", [128, 8], F32, st)
        den = sb("den", [128, 1], F32, st)
        den2 = sb("den2", [128, 1], F32, st)
        rw_bf = sb("rw_bf", [128, 8], BF16, st)
        rwt = sb("rwt", [8, 128], BF16, st)
        re_sb = sb("re_sb", [128, T], BF16, st)
        p_lg = psum("p_lg", [128, 8], F32, st)
        p_rt = psum("p_rt", [8, 128], BF16, st)
        p_ex = psum("p_ex", [128, 128], F32, st)
        p_h = [psum("p_h%d" % i, [128, 512], F32, st) for i in range(2)]

        def f_sync(eng):
            C.dma(eng, ab32[:, 0:256], wg_col, "ld_p6")
            for hf in range(2):
                C.wait(eng, "sc_p6", hf + 1)
                C.dma(eng, ab32[:, 0:2048], a_col[:, hf * 2048:(hf + 1) * 2048], "ld_p6")
            for hf in range(2):
                C.wait(eng, "sc_p6", hf + 3)
                C.dma(eng, ab32[:, 0:2048], b_re[:, hf * 2048:(hf + 1) * 2048], "ld_p6")
        blk.sync(f_sync)

        def f_scalar(eng):
            C.wait(eng, "ld_p6", 16)
            C.inc(eng.activation(out=wg_bf[:], in_=ab32[:, 0:256].rearrange("p (c e) -> p c e", e=8),
                                 func=AF.Copy), "sc_p6")
            for hf in range(2):
                C.wait(eng, "ld_p6", 16 * (hf + 2))
                C.inc(eng.activation(out=a_bf[:, hf * 16:(hf + 1) * 16, :],
                                     in_=ab32[:, 0:2048].rearrange("p (c d) -> p c d", d=128),
                                     func=AF.Copy), "sc_p6")
            for hf in range(2):
                C.wait(eng, "ld_p6", 16 * (hf + 4))
                C.inc(eng.activation(out=b_bf[:, hf * 2048:(hf + 1) * 2048],
                                     in_=ab32[:, 0:2048], func=AF.Copy), "sc_p6")
            for tt in range(16):
                C.wait(eng, "ve_rt", tt * 10 + 4)
                C.inc(eng.activation(out=expt[:], in_=lg[:], func=AF.Exp, bias=nm1[:]), "sc_rt")
        blk.scalar(f_scalar)

        def f_pe(eng):
            C.wait(eng, "sc_p6", 1)
            for tt in range(16):
                C.wait(eng, "ve_rt", tt * 10)
                C.wait(eng, "ve_re", tt)
                for c in range(NCH):
                    inst = eng.matmul(p_lg[:], xn2[:, c, tt * 128:(tt + 1) * 128], wg_bf[:, c, :],
                                      start=(c == 0), stop=(c == NCH - 1))
                C.inc(inst, "pe_lg")
                C.wait(eng, "ve_rt", tt * 10 + 9)
                C.inc(eng.transpose(p_rt[:], rw_bf[:], ident_bf[:]), "pe_lg")
                C.wait(eng, "ve_rt", tt * 10 + 10)
                C.inc(eng.matmul(p_ex[:], mexp_bf[:], rwt[:], start=True, stop=True), "pe_lg")
            C.wait(eng, "sc_p6", 3)
            C.wait(eng, "ve_re", 16)
            for ts_ in range(4):
                C.wait(eng, "ve_hw", ts_ - 1)
                for c in range(NCH):
                    inst = eng.matmul(p_h[ts_ % 2][:], a_bf[:, c, :], xn2[:, c, ts_ * 512:(ts_ + 1) * 512],
                                      start=(c == 0), stop=(c == NCH - 1))
                C.inc(inst, "pe_h")
        blk.tensor(f_pe)

        def f_vec(eng):
            for tt in range(16):
                C.wait(eng, "pe_lg", tt * 3 + 1)
                i1 = eng.tensor_copy(lg[:], p_lg[:]); C.inc(i1, "ve_rt"); C.waitn(eng, "ve_rt")
                i1 = eng.reduce_max(nm1[:], lg[:], axis=AX.X, negate=True); C.inc(i1, "ve_rt"); C.waitn(eng, "ve_rt")
                i1 = eng.tensor_scalar_mul(m1[:], nm1[:], -1.0); C.inc(i1, "ve_rt"); C.waitn(eng, "ve_rt")
                i1 = eng.tensor_scalar(mask1[:], lg[:], m1[:], None, ALU.is_ge); C.inc(i1, "ve_rt"); C.waitn(eng, "ve_rt")
                i1 = eng.scalar_tensor_tensor(lmask[:], mask1[:], -1e30, lg[:], ALU.mult, ALU.add); C.inc(i1, "ve_rt"); C.waitn(eng, "ve_rt")
                i1 = eng.reduce_max(m2r[:], lmask[:], axis=AX.X); C.inc(i1, "ve_rt"); C.waitn(eng, "ve_rt")
                i1 = eng.tensor_scalar(rwun[:], lg[:], m2r[:], None, ALU.is_ge); C.inc(i1, "ve_rt"); C.waitn(eng, "ve_rt")
                C.inc(eng.nop(), "ve_rt")
                C.wait(eng, "sc_rt", tt + 1)
                C.vchain(eng, eng.tensor_mul(rwun[:], rwun[:], expt[:]))
                C.vchain(eng, eng.reduce_sum(den[:], rwun[:], axis=AX.X))
                C.vchain(eng, eng.reciprocal(den2[:], den[:]))
                C.vchain(eng, eng.tensor_scalar_mul(rwun[:], rwun[:], den2[:]))
                C.inc(eng.tensor_copy(rw_bf[:], rwun[:]), "ve_rt")
                C.wait(eng, "pe_lg", tt * 3 + 2)
                C.inc(eng.tensor_copy(rwt[:], p_rt[:]), "ve_rt")
                C.wait(eng, "pe_lg", tt * 3 + 3)
                C.inc(eng.tensor_copy(re_sb[:, tt * 128:(tt + 1) * 128], p_ex[:]), "ve_re")
            for ts_ in range(4):
                C.wait(eng, "pe_h", ts_ + 1)
                C.inc(eng.tensor_mul(hw_bf[:, ts_ * 512:(ts_ + 1) * 512], p_h[ts_ % 2][:],
                                     re_sb[:, ts_ * 512:(ts_ + 1) * 512]), "ve_hw")
        blk.vector(f_vec)
        st.close()
    run_phase(p6)
    if dbg:
        def f_dbg2(eng):
            C.dma(eng, dbg_go, go_col[:], "st_dbg")
            C.dma(eng, dbg_hw, hw_bf[:], "st_dbg")
            for c2 in range(NCH):
                C.dma(eng, dbg_xn2.rearrange("(c p) t -> p c t", p=128)[:, c2, :], xn2[:, c2, :], "st_dbg")
            eng.wait_ge(C.sem("st_dbg"), C.n["st_dbg"])
        blk.gpsimd(f_dbg2)
        barrier()

    def p7():
        st = ExitStack()
        wb_ring = sb("wb_ring", [128, 4, 256], F32, st)
        wb_s = sb("wb_s", [128, NCH, 256], BF16, st)
        y1tile = sb("y1tile", [128, 2, 512], F32, st)
        y2t = sb("y2t", [128, 2, 512], F32, st)
        ot_t = sb("ot_t", [128, 2, 128], F32, st)
        p_f = [psum("p_f%d" % i, [128, 512], F32, st) for i in range(2)]
        p_tr = [psum("p_tr%d" % i, [128, 128], F32, st) for i in range(2)]

        def f_sync(eng):
            for hs in range(16):
                for c in range(NCH):
                    k = hs * NCH + c
                    C.wait(eng, "sc_b", k - 3)
                    C.dma(eng, wb_ring[:, k % 4, :],
                          wb_t[c * 128:(c + 1) * 128, hs * 256:(hs + 1) * 256], "ld_wb%d" % (k % 4))
                for j in range(2):
                    for ts_ in range(4):
                        u = hs * 8 + j * 4 + ts_
                        ot = hs * 2 + j
                        C.wait(eng, "ve_y2", u - 1)
                        C.dma(eng, y1tile[:, u % 2, :],
                              y1s_d[ot * 128:(ot + 1) * 128, ts_ * 512:(ts_ + 1) * 512], "ld_y1t%d" % (u % 2))
        blk.sync(f_sync)

        def f_scalar(eng):
            for hs in range(16):
                for c in range(NCH):
                    k = hs * NCH + c
                    C.wait(eng, "ld_wb%d" % (k % 4), 16 * (k // 4 + 1))
                    if c == 0:
                        C.wait(eng, "pe_f", hs * 8)
                    C.inc(eng.activation(out=wb_s[:, c, :], in_=wb_ring[:, k % 4, :], func=AF.Copy), "sc_b")
        blk.scalar(f_scalar)

        def trans_mm(eng, u):
            for sseg in range(4):
                C.wait(eng, "ve_ot", u * 4 + sseg - 1)
                C.inc(eng.transpose(p_tr[sseg % 2][:], y2t[:, u % 2, sseg * 128:(sseg + 1) * 128],
                                    ident[:]), "pe_tr")

        def f_pe(eng):
            for hs in range(16):
                C.wait(eng, "sc_b", (hs + 1) * NCH)
                for j in range(2):
                    ot = hs * 2 + j
                    for ts_ in range(4):
                        u = hs * 8 + j * 4 + ts_
                        C.wait(eng, "ve_y2", u - 1)
                        for c in range(NCH):
                            eng.matmul(p_f[u % 2][:], wb_s[:, c, j * 128:(j + 1) * 128],
                                       xn2[:, c, ts_ * 512:(ts_ + 1) * 512],
                                       start=(c == 0), stop=False)
                        inst = eng.matmul(p_f[u % 2][:], b_bf[:, ot * 128:(ot + 1) * 128],
                                          hw_bf[:, ts_ * 512:(ts_ + 1) * 512], start=False, stop=True)
                        C.inc(inst, "pe_f")
                        if u >= 1:
                            C.wait(eng, "ve_y2", u)
                            trans_mm(eng, u - 1)
            C.wait(eng, "ve_y2", 128)
            trans_mm(eng, 127)
        blk.tensor(f_pe)

        def f_vec(eng):
            for hs in range(16):
                for j in range(2):
                    ot = hs * 2 + j
                    for ts_ in range(4):
                        u = hs * 8 + j * 4 + ts_
                        C.wait(eng, "pe_f", u + 1)
                        C.wait(eng, "ld_y1t%d" % (u % 2), 16 * (u // 2 + 1))
                        C.wait(eng, "pe_tr", 4 * (u - 1))
                        C.inc(eng.scalar_tensor_tensor(y2t[:, u % 2, :], p_f[u % 2][:],
                                                       sc_bb[:, ot:ot + 1], y1tile[:, u % 2, :],
                                                       ALU.add, ALU.add), "ve_y2")
                        if u >= 1:
                            up = u - 1
                            for sseg in range(4):
                                idx = up * 4 + sseg
                                C.wait(eng, "pe_tr", idx + 1)
                                C.wait(eng, "st_out%d" % (idx % 2), 16 * (idx // 2))
                                C.inc(eng.tensor_copy(ot_t[:, sseg % 2, :], p_tr[sseg % 2][:]), "ve_ot")
            up = 127
            for sseg in range(4):
                idx = up * 4 + sseg
                C.wait(eng, "pe_tr", idx + 1)
                C.wait(eng, "st_out%d" % (idx % 2), 16 * (idx // 2))
                C.inc(eng.tensor_copy(ot_t[:, sseg % 2, :], p_tr[sseg % 2][:]), "ve_ot")
        blk.vector(f_vec)

        def f_g(eng):
            for u in range(128):
                hs = u // 8
                j = (u % 8) // 4
                ts_ = u % 4
                ot = hs * 2 + j
                for sseg in range(4):
                    C.wait(eng, "ve_ot", u * 4 + sseg + 1)
                    row = ts_ * 512 + sseg * 128
                    C.dma(eng, out[row:row + 128, ot * 128:(ot + 1) * 128],
                          ot_t[:, sseg % 2, :], "st_out%d" % ((u * 4 + sseg) % 2))
            C.wait(eng, "st_out0", 16 * 256)
            C.wait(eng, "st_out1", 16 * 256)
        blk.gpsimd(f_g)
        st.close()
    run_phase(p7)

    blk.__exit__(None, None, None)
    p67_stack.close()
    es.close()
    return nc


def kernel(L0_attn_hidden_states, L0_attn_q_weight, L0_attn_q_scale,
           L0_attn_k_weight, L0_attn_k_scale, L0_attn_v_weight, L0_attn_v_scale,
           L0_attn_o_weight, L0_attn_o_scale, L0_ffn_W_base, L0_ffn_b_base,
           L0_ffn_W_gate, L0_ffn_A, L0_ffn_B):
    x = np.asarray(L0_attn_hidden_states, dtype=np.float32)
    wq = np.asarray(L0_attn_q_weight, np.float32)
    wk = np.asarray(L0_attn_k_weight, np.float32)
    wv = np.asarray(L0_attn_v_weight, np.float32)
    wo = np.asarray(L0_attn_o_weight, np.float32)
    wb = np.asarray(L0_ffn_W_base, np.float32)
    wg = np.asarray(L0_ffn_W_gate, np.float32)
    A = np.asarray(L0_ffn_A, np.float32)
    B = np.asarray(L0_ffn_B, np.float32)

    shared = dict(
        wq_t=np.ascontiguousarray(wq.T),
        wk_t=np.ascontiguousarray(wk.T),
        wv_t=np.ascontiguousarray(wv.T),
        wo_t=np.ascontiguousarray(wo.T),
        wb_t=np.ascontiguousarray(wb.T),
        wg_col=np.ascontiguousarray(wg.T.reshape(32, 128, 8).transpose(1, 0, 2).reshape(128, 256)),
        a_col=np.ascontiguousarray(A.transpose(2, 0, 1).reshape(4096, 128).reshape(32, 128, 128).transpose(1, 0, 2).reshape(128, 4096)),
        b_re=np.ascontiguousarray(B.transpose(0, 2, 1).reshape(128, 4096)),
        qs_col=np.ascontiguousarray(np.asarray(L0_attn_q_scale, np.float32).reshape(32, 128).T),
        ks_col=np.ascontiguousarray(np.asarray(L0_attn_k_scale, np.float32).reshape(8, 128).T),
        vs_col=np.ascontiguousarray(np.asarray(L0_attn_v_scale, np.float32).reshape(8, 128).T),
        os_col=np.ascontiguousarray(np.asarray(L0_attn_o_scale, np.float32).reshape(32, 128).T),
        bb_col=np.ascontiguousarray(np.asarray(L0_ffn_b_base, np.float32).reshape(32, 128).T),
        ident_in=np.eye(128, dtype=np.float32),
        mexp_in=np.kron(np.eye(8), np.ones((1, 16))).astype(np.float32).reshape(8, 128),
    )

    xt = np.ascontiguousarray(x.transpose(0, 2, 1))
    in_maps = []
    for core in range(8):
        b, h = core // 2, core % 2
        own = xt[b][:, h * T:(h + 1) * T]
        oth = xt[b][:, (1 - h) * T:(2 - h) * T]
        m = dict(shared)
        m["x_all"] = np.ascontiguousarray(np.concatenate([own, oth], axis=1))
        in_maps.append(m)

    nc = build_nc()
    res = run_bass_kernel_spmd(nc, in_maps, core_ids=list(range(8)))

    outp = np.empty((4, 4096, 4096), np.float32)
    for core in range(8):
        b, h = core // 2, core % 2
        outp[b, h * T:(h + 1) * T, :] = res.results[core]["out"]
    return outp


# version 21
# speedup vs baseline: 1.0377x; 1.0377x over previous
import numpy as np
import ml_dtypes
import concourse.bass as bass
import concourse.mybir as mybir
from concourse.bass_utils import run_bass_kernel_spmd
from contextlib import ExitStack

F32 = mybir.dt.float32
BF16 = mybir.dt.bfloat16
AF = mybir.ActivationFunctionType
ALU = mybir.AluOpType
AX = mybir.AxisListType

D = 4096
S = 4096
T = 2048
NCH = 32
NH = 32
NKV = 8
HD = 128
KV = 1024
EPS = 1e-5
ENGS = ["sync", "tensor", "vector", "scalar", "gpsimd"]


class Ctx:

    def __init__(self, nc, es):
        self.nc = nc
        self.es = es
        self.h = {}
        self.n = {}
        self.base = {}
        self.pool = []
        self.keep = set()
        self.bar_n = 0

    def sem(self, name):
        if name not in self.h:
            if self.pool:
                h, b = self.pool.pop()
            else:
                h = self.nc.alloc_semaphore(name=name)
                b = 0
            self.h[name] = h
            self.base[name] = b
            self.n[name] = b
        return self.h[name]

    def release_phase(self, before):
        for name in list(self.h):
            if name not in before and name not in self.keep:
                self.pool.append((self.h.pop(name), self.n.pop(name)))
                del self.base[name]

    def inc(self, inst, name, k=1):
        inst.then_inc(self.sem(name), k)
        self.n[name] += k
        return self.n[name]

    def dma(self, eng, out, in_, name):
        eng.dma_start(out=out, in_=in_).then_inc(self.sem(name), 16)
        self.n[name] += 16

    def dma_big(self, eng, out, in_, name):
        h = out.shape[1] // 2
        eng.dma_start(out=out[:, 0:h, :], in_=in_[:, 0:h, :]).then_inc(self.sem(name), 16)
        eng.dma_start(out=out[:, h:2 * h, :], in_=in_[:, h:2 * h, :]).then_inc(self.sem(name), 16)
        self.n[name] += 32

    def wait(self, eng, name, val):
        if val > 0:
            eng.wait_ge(self.sem(name), self.base[name] + val)

    def waitn(self, eng, name):
        eng.wait_ge(self.sem(name), self.n[name])

    def vchain(self, eng, inst):
        self.inc(inst, "vv")
        eng.wait_ge(self.sem("vv"), self.n["vv"])


def build_nc(dbg=False):
    nc = bass.Bass("TRN2", target_bir_lowering=False, debug=False,
                   dynamic_dma_scratch_size=16384)
    KIND = "ExternalOutput" if dbg else "Internal"
    es = ExitStack()
    C = Ctx(nc, es)

    x_all = nc.dram_tensor("x_all", [D, S], F32, kind="ExternalInput").ap()
    wq_t = nc.dram_tensor("wq_t", [D, D], F32, kind="ExternalInput").ap()
    wk_t = nc.dram_tensor("wk_t", [D, KV], F32, kind="ExternalInput").ap()
    wv_t = nc.dram_tensor("wv_t", [D, KV], F32, kind="ExternalInput").ap()
    wo_t = nc.dram_tensor("wo_t", [D, D], F32, kind="ExternalInput").ap()
    wb_t = nc.dram_tensor("wb_t", [D, D], F32, kind="ExternalInput").ap()
    wg_col = nc.dram_tensor("wg_col", [128, NCH * 8], F32, kind="ExternalInput").ap()
    a_col = nc.dram_tensor("a_col", [128, NCH * 128], F32, kind="ExternalInput").ap()
    b_re = nc.dram_tensor("b_re", [128, D], F32, kind="ExternalInput").ap()
    qs_col_d = nc.dram_tensor("qs_col", [128, 32], F32, kind="ExternalInput").ap()
    ks_col_d = nc.dram_tensor("ks_col", [128, 8], F32, kind="ExternalInput").ap()
    vs_col_d = nc.dram_tensor("vs_col", [128, 8], F32, kind="ExternalInput").ap()
    os_col_d = nc.dram_tensor("os_col", [128, 32], F32, kind="ExternalInput").ap()
    bb_col_d = nc.dram_tensor("bb_col", [128, 32], F32, kind="ExternalInput").ap()
    ident_in = nc.dram_tensor("ident_in", [128, 128], F32, kind="ExternalInput").ap()
    mexp_in = nc.dram_tensor("mexp_in", [8, 128], F32, kind="ExternalInput").ap()

    out = nc.dram_tensor("out", [T, D], F32, kind="ExternalOutput").ap()

    xn1s = nc.dram_tensor("xn1s", [D, S], BF16, kind=KIND).ap()
    ks_d = nc.dram_tensor("ks_d", [KV, S], BF16, kind=KIND).ap()
    vs_d = nc.dram_tensor("vs_d", [S, KV], BF16, kind=KIND).ap()
    qs_d = nc.dram_tensor("qs_d", [D, T], BF16, kind=KIND).ap()
    ats_d = nc.dram_tensor("ats_d", [D, T], BF16, kind=KIND).ap()
    y1s_d = nc.dram_tensor("y1s_d", [D, T], F32, kind=KIND).ap()
    if dbg:
        dbg_call = nc.dram_tensor("dbg_call", [128, NH], F32, kind="ExternalOutput").ap()
        dbg_gq = nc.dram_tensor("dbg_gq", [128, 32], F32, kind="ExternalOutput").ap()
        dbg_gv = nc.dram_tensor("dbg_gv", [128, 8], F32, kind="ExternalOutput").ap()
        dbg_go = nc.dram_tensor("dbg_go", [128, 32], F32, kind="ExternalOutput").ap()
        dbg_hw = nc.dram_tensor("dbg_hw", [128, T], BF16, kind="ExternalOutput").ap()
        dbg_xn2 = nc.dram_tensor("dbg_xn2", [D, T], BF16, kind="ExternalOutput").ap()
        dbg_gk = nc.dram_tensor("dbg_gk", [128, 8], F32, kind="ExternalOutput").ap()
        dbg_sq = nc.dram_tensor("dbg_sq", [128, 2], F32, kind="ExternalOutput").ap()
        dbg_qsub = nc.dram_tensor("dbg_qsub", [128, NH * 128], BF16, kind="ExternalOutput").ap()
        dbg_sk = nc.dram_tensor("dbg_sk", [128, 8], F32, kind="ExternalOutput").ap()
        dbg_wk = nc.dram_tensor("dbg_wk", [128, KV], BF16, kind="ExternalOutput").ap()

    def sb(name, shape, dt, stack=None, side=None):
        return (stack or es).enter_context(nc.sbuf_tensor(name, shape, dt, side=side))

    def psum(name, shape, dt=F32, stack=None):
        return (stack or es).enter_context(nc.psum_tensor(name, shape, dt))

    blk = nc.Block().__enter__()
    C.keep = {"vv", "st_dbg"} | {"bar_" + e for e in ENGS}

    def run_phase(fn):
        before = set(C.h)
        with nc.named_scope(fn.__name__):
            fn()
            barrier()
        C.release_phase(before)

    def barrier():
        C.bar_n += 1
        cnum = C.bar_n
        for e in ENGS:
            def body(eng, e=e, cnum=cnum):
                if e != "sync":
                    eng.drain()
                C.inc(eng.nop(), "bar_" + e)
                for o in ENGS:
                    if o != e:
                        C.wait(eng, "bar_" + o, cnum)
            getattr(blk, e)(body)

    ident = sb("ident", [128, 128], F32)
    ident_bf = sb("ident_bf", [128, 128], BF16)
    mexp_bf = sb("mexp_bf", [8, 128], BF16)
    ones_col = sb("ones_col", [128, 1], BF16)
    ones_colf = sb("ones_colf", [128, 1], F32)
    ones_row = sb("ones_row", [1, 128], F32)
    sc_qs = sb("sc_qs", [128, 32], F32)
    sc_ks = sb("sc_ks", [128, 8], F32)
    sc_vs = sb("sc_vs", [128, 8], F32)
    sc_os = sb("sc_os", [128, 32], F32)
    sc_bb = sb("sc_bb", [128, 32], F32)
    gq_col = sb("gq_col", [128, 32], F32)
    gv_col = sb("gv_col", [128, 8], F32)
    go_col = sb("go_col", [128, 32], F32)
    gk_comb = sb("gk_comb", [128, 8], F32)
    c_all = sb("c_all", [128, NH], F32)
    eps1 = sb("eps1", [1, 1], F32)

    def p0_consts():
        mexp32 = sb("mexp32", [8, 128], F32)

        def f_sync(eng):
            C.dma(eng, ident[:], ident_in, "ld0")
            C.dma(eng, mexp32[:], mexp_in, "ld0")
            C.dma(eng, sc_qs[:], qs_col_d, "ld0")
            C.dma(eng, sc_ks[:], ks_col_d, "ld0")
            C.dma(eng, sc_vs[:], vs_col_d, "ld0")
            C.dma(eng, sc_os[:], os_col_d, "ld0")
            C.dma(eng, sc_bb[:], bb_col_d, "ld0")
        blk.sync(f_sync)

        def f_vec(eng):
            C.wait(eng, "ld0", 7 * 16)
            eng.tensor_copy(ident_bf[:], ident[:])
            eng.tensor_copy(mexp_bf[:], mexp32[:])
            eng.memset(ones_col[:], 1.0)
            eng.memset(ones_colf[:], 1.0)
            eng.memset(ones_row[:], 1.0)
            eng.memset(eps1[:], EPS)
        blk.vector(f_vec)
    run_phase(p0_consts)

    def prep_bitnet(w_t, o_cols, w_s, s_fin, tag):
        n_ot = o_cols // 128
        ist = ExitStack()
        ring = sb(tag + "_ring", [128, 2, o_cols], F32, ist)
        absb = sb(tag + "_abs", [128, 2, o_cols], BF16, ist)
        s_ps = [psum(tag + "_sps%d" % j, [128, 1], F32, ist) for j in range(n_ot)]
        ld, sg, ab, pq = tag + "_ld", tag + "_sg", tag + "_ab", tag + "_pq"

        def f_sync(eng):
            for c in range(NCH):
                C.wait(eng, sg, c - 1)
                C.wait(eng, ab, c - 1)
                C.dma(eng, ring[:, c % 2, :], w_t[c * 128:(c + 1) * 128, 0:o_cols], ld + str(c % 2))
        blk.sync(f_sync)

        def f_scalar(eng):
            for c in range(NCH):
                C.wait(eng, ld + str(c % 2), 16 * (c // 2 + 1))
                C.inc(eng.activation(out=w_s[:, c, :], in_=ring[:, c % 2, :], func=AF.Sign), sg)
                C.wait(eng, pq, c - 1)
                C.inc(eng.activation(out=absb[:, c % 2, :], in_=ring[:, c % 2, :], func=AF.Abs), ab)
        blk.scalar(f_scalar)

        def f_pe(eng):
            for c in range(NCH):
                C.wait(eng, ab, c + 1)
                for j in range(n_ot):
                    inst = eng.matmul(s_ps[j][:], absb[:, c % 2, j * 128:(j + 1) * 128],
                                      ones_col[:], start=(c == 0), stop=(c == NCH - 1))
                C.inc(inst, pq)
        blk.tensor(f_pe)

        def f_vec2(eng):
            C.wait(eng, pq, NCH)
            for j in range(n_ot):
                inst = eng.tensor_scalar(s_fin[:, j:j + 1], s_ps[j][:], 1.0 / D, 1e-8, ALU.mult, ALU.max)
            C.inc(inst, tag + "_done")
        blk.vector(f_vec2)
        ist.close()

    def p1a():
        st = ExitStack()
        wv_s = sb("wv_s", [128, NCH, KV], BF16, st)
        sv_fin = sb("sv_fin", [128, 8], F32, st)
        prep_bitnet(wv_t, KV, wv_s, sv_fin, "pv")

        def f_vgv(eng):
            C.wait(eng, "pv_done", 1)
            C.inc(eng.tensor_mul(gv_col[:], sv_fin[:], sc_vs[:]), "gv_done")
        blk.vector(f_vgv)
        barrier()

        SLAB = 256
        NSL = S // SLAB
        xsl = sb("xsl", [128, NCH, SLAB], F32, st)
        xsq = sb("xsq", [128, NCH, SLAB], BF16, st)
        xn_sl = sb("xn_sl", [128, 2, NCH, SLAB], BF16, st)
        m_row = sb("m_row", [1, SLAB], F32, st)
        v_row = sb("v_row", [1, SLAB], F32, st)
        sd_row = sb("sd_row", [1, SLAB], F32, st)
        r_row = sb("r_row", [1, SLAB], F32, st)
        m_b = sb("m_b", [128, SLAB], F32, st)
        r_b = sb("r_b", [128, SLAB], F32, st)
        tmp_a = sb("tmp_a", [128, SLAB], F32, st)
        tmp_b = sb("tmp_b", [128, SLAB], F32, st)
        vev = sb("vev", [128, 2, KV], BF16, st)
        p_sum = psum("p_sum", [1, SLAB], F32, st)
        p_sq = psum("p_sq", [1, SLAB], F32, st)
        p_mb = psum("p_mb", [128, SLAB], F32, st)
        p_rb = psum("p_rb", [128, SLAB], F32, st)
        p_v = [psum("p_v%d" % i, [128, 512], F32, st) for i in range(4)]
        x_re = x_all.rearrange("(c p) t -> p c t", p=128)
        xn_re = xn1s.rearrange("(c p) t -> p c t", p=128)

        def f_sync(eng):
            for sl in range(NSL):
                C.wait(eng, "ve_ap", sl)
                C.dma_big(eng, xsl[:], x_re[:, :, sl * SLAB:(sl + 1) * SLAB], "ld_x")
        blk.sync(f_sync)

        def f_scalar(eng):
            for sl in range(NSL):
                C.wait(eng, "ld_x", 32 * (sl + 1))
                C.wait(eng, "pe_st", sl)
                for c in range(NCH):
                    inst = eng.activation(out=xsq[:, c, :], in_=xsl[:, c, :], func=AF.Square)
                C.inc(inst, "sc_cast")
                C.wait(eng, "ve_var", sl + 1)
                C.inc(eng.activation(out=sd_row[:], in_=v_row[:], func=AF.Sqrt, bias=eps1[:]), "sc_sd")
        blk.scalar(f_scalar)

        def f_pe(eng):
            C.wait(eng, "gv_done", 1)
            for sl in range(NSL):
                C.wait(eng, "sc_cast", sl + 1)
                for c in range(NCH):
                    eng.matmul(p_sum[:], ones_colf[:], xsl[:, c, :], start=(c == 0), stop=(c == NCH - 1))
                    inst = eng.matmul(p_sq[:], ones_col[:], xsq[:, c, :], start=(c == 0), stop=(c == NCH - 1))
                C.inc(inst, "pe_st")
                C.wait(eng, "ve_mr", sl + 1)
                eng.matmul(p_mb[:], ones_row[:], m_row[:], start=True, stop=True)
                inst = eng.matmul(p_rb[:], ones_row[:], r_row[:], start=True, stop=True)
                C.inc(inst, "pe_bc")
                C.wait(eng, "ve_ap", sl + 1)
                for tt in range(2):
                    for hf in range(2):
                        pvt = p_v[tt * 2 + hf]
                        for c in range(NCH):
                            inst = eng.matmul(pvt[:], xn_sl[:, sl % 2, c, tt * 128:(tt + 1) * 128],
                                              wv_s[:, c, hf * 512:(hf + 1) * 512],
                                              start=(c == 0), stop=(c == NCH - 1))
                        C.inc(inst, "pe_v")
        blk.tensor(f_pe)

        def f_vec(eng):
            for sl in range(NSL):
                C.wait(eng, "pe_st", sl + 1)
                eng.tensor_scalar_mul(m_row[:], p_sum[:], 1.0 / D)
                C.vchain(eng, eng.tensor_scalar_mul(v_row[:], p_sq[:], 1.0 / D))
                C.vchain(eng, eng.tensor_mul(tmp_a[0:1, :], m_row[:], m_row[:]))
                C.inc(eng.tensor_sub(v_row[:], v_row[:], tmp_a[0:1, :]), "ve_var")
                C.wait(eng, "sc_sd", sl + 1)
                C.inc(eng.reciprocal(r_row[:], sd_row[:]), "ve_mr")
                C.wait(eng, "pe_bc", sl + 1)
                eng.tensor_copy(m_b[:], p_mb[:])
                C.vchain(eng, eng.tensor_copy(r_b[:], p_rb[:]))
                C.wait(eng, "st_xn", 32 * (sl - 1))
                C.wait(eng, "pe_v", 4 * sl)
                tmps = [tmp_a, tmp_b]
                eng.tensor_sub(tmps[0][:], xsl[:, 0, :], m_b[:])
                for c in range(1, NCH):
                    eng.tensor_sub(tmps[c % 2][:], xsl[:, c, :], m_b[:])
                    eng.tensor_mul(xn_sl[:, sl % 2, c - 1, :], tmps[(c - 1) % 2][:], r_b[:])
                inst = eng.tensor_mul(xn_sl[:, sl % 2, NCH - 1, :], tmps[(NCH - 1) % 2][:], r_b[:])
                C.inc(inst, "ve_ap")
                for tt in range(2):
                    C.wait(eng, "st_v%d" % tt, 16 * ((sl * 2 + tt) // 2))
                    for hf in range(2):
                        C.wait(eng, "pe_v", sl * 4 + tt * 2 + hf + 1)
                        inst = eng.tensor_copy(vev[:, tt, hf * 512:(hf + 1) * 512], p_v[tt * 2 + hf][:])
                    C.inc(inst, "ve_vev")
        blk.vector(f_vec)

        def f_g(eng):
            for sl in range(NSL):
                C.wait(eng, "ve_ap", sl + 1)
                C.dma_big(eng, xn_re[:, :, sl * SLAB:(sl + 1) * SLAB], xn_sl[:, sl % 2, :, :], "st_xn")
                for tt in range(2):
                    C.wait(eng, "ve_vev", sl * 2 + tt + 1)
                    row = sl * SLAB + tt * 128
                    C.dma(eng, vs_d[row:row + 128, :], vev[:, tt, :], "st_v%d" % tt)
            C.wait(eng, "st_xn", 32 * NSL)
            C.wait(eng, "st_v0", 16 * NSL)
            C.wait(eng, "st_v1", 16 * NSL)
        blk.gpsimd(f_g)
        st.close()
    run_phase(p1a)

    ksub_stack = ExitStack()
    K_sub = sb("K_sub", [128, NKV * 512], BF16, ksub_stack)
    Q_sub = sb("Q_sub", [128, NH * 128], BF16, ksub_stack)

    def p1b():
        st = ExitStack()
        wk_s = sb("wk_s", [128, NCH, KV], BF16, st)
        sk_fin = sb("sk_fin", [128, 8], F32, st)
        prep_bitnet(wk_t, KV, wk_s, sk_fin, "pk")

        def f_vgk(eng):
            C.wait(eng, "pk_done", 1)
            C.inc(eng.scalar_tensor_tensor(gk_comb[:], sk_fin[:], 1.0 / np.sqrt(HD), sc_ks[:],
                                           ALU.mult, ALU.mult), "gk_done")
        blk.vector(f_vgk)
        barrier()

        SLAB = 512
        NSL = S // SLAB
        xnsl = sb("xnsl", [128, 2, NCH, SLAB], BF16, st)
        kev = sb("kev", [128, 2, SLAB], BF16, st)
        p_k = [psum("p_k%d" % i, [128, SLAB], F32, st) for i in range(2)]
        xn_re = xn1s.rearrange("(c p) t -> p c t", p=128)

        def f_sync(eng):
            for sl in range(NSL):
                C.wait(eng, "pe_k", 8 * (sl - 1))
                C.dma_big(eng, xnsl[:, sl % 2, :, :], xn_re[:, :, sl * SLAB:(sl + 1) * SLAB], "ld_xn%d" % (sl % 2))
        blk.sync(f_sync)

        def f_pe(eng):
            C.wait(eng, "gk_done", 1)
            for sl in range(NSL):
                C.wait(eng, "ld_xn%d" % (sl % 2), 32 * (sl // 2 + 1))
                for j in range(8):
                    C.wait(eng, "ve_k", sl * 8 + j - 1)
                    for c in range(NCH):
                        inst = eng.matmul(p_k[j % 2][:], wk_s[:, c, j * 128:(j + 1) * 128],
                                          xnsl[:, sl % 2, c, :], start=(c == 0), stop=(c == NCH - 1))
                    C.inc(inst, "pe_k")
        blk.tensor(f_pe)

        def f_vec(eng):
            for sl in range(NSL):
                for j in range(8):
                    C.wait(eng, "pe_k", sl * 8 + j + 1)
                    C.wait(eng, "st_k%d" % (j % 2), 16 * ((sl * 8 + j) // 2))
                    C.vchain(eng, eng.tensor_copy(kev[:, j % 2, :], p_k[j % 2][:]))
                    C.inc(eng.tensor_copy(K_sub[:, j * 512 + sl * 64: j * 512 + (sl + 1) * 64],
                                          kev[:, j % 2, ::8]), "ve_k")
        blk.vector(f_vec)

        def f_g(eng):
            for sl in range(NSL):
                for j in range(8):
                    C.wait(eng, "ve_k", sl * 8 + j + 1)
                    C.dma(eng, ks_d[j * 128:(j + 1) * 128, sl * SLAB:(sl + 1) * SLAB],
                          kev[:, j % 2, :], "st_k%d" % (j % 2))
            C.wait(eng, "st_k0", 16 * NSL * 4)
            C.wait(eng, "st_k1", 16 * NSL * 4)
            if dbg:
                C.dma(eng, dbg_sk, sk_fin[:], "st_dbg")
                C.dma(eng, dbg_wk, wk_s[:, 0, :], "st_dbg")
                eng.wait_ge(C.sem("st_dbg"), C.n["st_dbg"])
        blk.gpsimd(f_g)
        st.close()
    run_phase(p1b)

    def p2():
        st = ExitStack()
        xn_own = sb("xn_own", [128, NCH, T], BF16, st)
        wq_ring = sb("wq_ring", [128, 4, 256], F32, st)
        wq_s = sb("wq_s", [128, NCH, 256], BF16, st)
        absb = sb("p2_abs", [128, 2, 256], BF16, st)
        sq_fin = sb("sq_fin", [128, 2], F32, st)
        qev = sb("qev", [128, 2, 512], BF16, st)
        p_sq2 = [psum("p_sq2_%d" % j, [128, 1], F32, st) for j in range(2)]
        p_q = [psum("p_q%d" % i, [128, 512], F32, st) for i in range(2)]
        xn_re = xn1s.rearrange("(c p) t -> p c t", p=128)

        def f_sync(eng):
            for i in range(4):
                C.dma_big(eng, xn_own[:, :, i * 512:(i + 1) * 512],
                          xn_re[:, :, i * 512:(i + 1) * 512], "ld_xo")
            for hs in range(16):
                for c in range(NCH):
                    k = hs * NCH + c
                    C.wait(eng, "sc_q", k - 3)
                    C.wait(eng, "ve_qab", k - 3)
                    C.dma(eng, wq_ring[:, k % 4, :],
                          wq_t[c * 128:(c + 1) * 128, hs * 256:(hs + 1) * 256], "ld_wq%d" % (k % 4))
        blk.sync(f_sync)

        def f_scalar(eng):
            for hs in range(16):
                for c in range(NCH):
                    k = hs * NCH + c
                    C.wait(eng, "ld_wq%d" % (k % 4), 16 * (k // 4 + 1))
                    if c == 0:
                        C.wait(eng, "pe_q", hs * 8)
                    C.inc(eng.activation(out=wq_s[:, c, :], in_=wq_ring[:, k % 4, :], func=AF.Sign), "sc_q")
                    C.wait(eng, "pe_sq2", k - 1)
                    C.inc(eng.activation(out=absb[:, k % 2, :], in_=wq_ring[:, k % 4, :], func=AF.Abs), "ve_qab")
        blk.scalar(f_scalar)

        def f_pe(eng):
            for hs in range(16):
                for c in range(NCH):
                    k = hs * NCH + c
                    C.wait(eng, "ve_qab", k + 1)
                    for j in range(2):
                        inst = eng.matmul(p_sq2[j][:], absb[:, k % 2, j * 128:(j + 1) * 128],
                                          ones_col[:], start=(c == 0), stop=(c == NCH - 1))
                    C.inc(inst, "pe_sq2")
                C.wait(eng, "sc_q", (hs + 1) * NCH)
                C.wait(eng, "ve_gq", hs + 1)
                if hs == 0:
                    C.wait(eng, "ld_xo", 128)
                for j in range(2):
                    for ts_ in range(4):
                        u = j * 4 + ts_
                        C.wait(eng, "ve_q", hs * 8 + u - 1)
                        for c in range(NCH):
                            inst = eng.matmul(p_q[u % 2][:], wq_s[:, c, j * 128:(j + 1) * 128],
                                              xn_own[:, c, ts_ * 512:(ts_ + 1) * 512],
                                              start=(c == 0), stop=(c == NCH - 1))
                        C.inc(inst, "pe_q")
        blk.tensor(f_pe)

        def f_vec(eng):
            for hs in range(16):
                C.wait(eng, "pe_sq2", (hs + 1) * NCH)
                for j in range(2):
                    inst = eng.tensor_scalar(sq_fin[:, j:j + 1], p_sq2[j][:], 1.0 / D, 1e-8, ALU.mult, ALU.max)
                C.vchain(eng, inst)
                for j in range(2):
                    ot = hs * 2 + j
                    inst = eng.scalar_tensor_tensor(gq_col[:, ot:ot + 1], sq_fin[:, j:j + 1],
                                                    sc_qs[:, ot:ot + 1], gk_comb[:, ot // 4:ot // 4 + 1],
                                                    ALU.mult, ALU.mult)
                C.inc(inst, "ve_gq")
                C.waitn(eng, "ve_gq")
                for j in range(2):
                    for ts_ in range(4):
                        u = j * 4 + ts_
                        ot = hs * 2 + j
                        C.wait(eng, "pe_q", hs * 8 + u + 1)
                        C.wait(eng, "st_q%d" % (u % 2), 16 * ((hs * 8 + u) // 2))
                        C.vchain(eng, eng.tensor_scalar_mul(qev[:, u % 2, :], p_q[u % 2][:], gq_col[:, ot:ot + 1]))
                        C.inc(eng.tensor_copy(Q_sub[:, ot * 128 + ts_ * 32: ot * 128 + (ts_ + 1) * 32],
                                              qev[:, u % 2, ::16]), "ve_q")
        blk.vector(f_vec)

        def f_g(eng):
            for hs in range(16):
                for j in range(2):
                    for ts_ in range(4):
                        u = j * 4 + ts_
                        ot = hs * 2 + j
                        C.wait(eng, "ve_q", hs * 8 + u + 1)
                        C.dma(eng, qs_d[ot * 128:(ot + 1) * 128, ts_ * 512:(ts_ + 1) * 512],
                              qev[:, u % 2, :], "st_q%d" % (u % 2))
            C.wait(eng, "st_q0", 16 * 64)
            C.wait(eng, "st_q1", 16 * 64)
            if dbg:
                C.dma(eng, dbg_gk, gk_comb[:], "st_dbg")
                C.dma(eng, dbg_sq, sq_fin[:], "st_dbg")
                C.dma(eng, dbg_qsub, Q_sub[:], "st_dbg")
                eng.wait_ge(C.sem("st_dbg"), C.n["st_dbg"])
        blk.gpsimd(f_g)
        st.close()
    run_phase(p2)

    def pc():
        st = ExitStack()
        cmax = sb("cmax", [128, NH], F32, st)
        tcm = sb("tcm", [32, 128], F32, st)
        negc_row = sb("negc_row", [1, 32], F32, st)
        p_cs = [psum("p_cs%d" % i, [128, 512], F32, st) for i in range(2)]
        p_t1 = psum("p_t1", [32, 128], F32, st)
        p_t2 = psum("p_t2", [1, 32], F32, st)
        p_ca = psum("p_ca", [128, 32], F32, st)

        def f_pe(eng):
            for h in range(NH):
                C.wait(eng, "ve_cm", h - 1)
                C.inc(eng.matmul(p_cs[h % 2][:], Q_sub[:, h * 128:(h + 1) * 128],
                                 K_sub[:, (h // 4) * 512:(h // 4 + 1) * 512],
                                 start=True, stop=True), "pe_cs")
            C.wait(eng, "ve_cm", NH)
            C.inc(eng.transpose(p_t1[:], cmax[:], ident[:]), "pe_cs")
            C.wait(eng, "ve_cm", NH + 1)
            C.inc(eng.transpose(p_t2[:], tcm[:, 0:1], ident[0:32, 0:32]), "pe_cs")
            C.wait(eng, "ve_cm", NH + 2)
            C.inc(eng.matmul(p_ca[:], ones_row[:], negc_row[:], start=True, stop=True), "pe_cs")
        blk.tensor(f_pe)

        def f_vec(eng):
            for h in range(NH):
                C.wait(eng, "pe_cs", h + 1)
                C.inc(eng.reduce_max(cmax[:, h:h + 1], p_cs[h % 2][:], axis=AX.X), "ve_cm")
            C.wait(eng, "pe_cs", NH + 1)
            C.inc(eng.reduce_max(tcm[:, 0:1], p_t1[:], axis=AX.X, negate=True), "ve_cm")
            C.wait(eng, "pe_cs", NH + 2)
            C.inc(eng.tensor_copy(negc_row[:], p_t2[:]), "ve_cm")
            C.wait(eng, "pe_cs", NH + 3)
            C.inc(eng.tensor_copy(c_all[:], p_ca[:]), "ve_ca")
        blk.vector(f_vec)
        st.close()
    run_phase(pc)
    if dbg:
        def f_dbg1(eng):
            C.dma(eng, dbg_call, c_all[:], "st_dbg")
            C.dma(eng, dbg_gq, gq_col[:], "st_dbg")
            C.dma(eng, dbg_gv, gv_col[:], "st_dbg")
            eng.wait_ge(C.sem("st_dbg"), C.n["st_dbg"])
        blk.gpsimd(f_dbg1)
        barrier()
    ksub_stack.close()

    def p3():
        st = ExitStack()
        kbuf = sb("kbuf", [128, 2, S], BF16, st)
        vbuf = sb("vbuf", [128, 2, NCH, 128], BF16, st)
        qbuf = sb("qbuf", [128, 2, T], BF16, st)
        pt = sb("ptile", [128, 6, 512], BF16, st)
        acc3 = sb("acc3", [128, 2, 512], F32, st)
        lnrow = sb("lnrow", [1, 512], F32, st)
        rec_row = sb("rec_row", [1, 512], F32, st)
        rec_b = sb("rec_b", [128, 512], F32, st)
        atile = sb("atile", [128, 2, 512], BF16, st)
        p_sc = [psum("p3_sc%d" % i, [128, 2, 512], F32, st) for i in range(3)]
        p_pv = psum("p3_pv", [128, 512], F32, st)
        p_mix = psum("p3_mix", [128, 512], F32, st)
        vs_re = vs_d.rearrange("(kt p) d -> p kt d", p=128)

        NSP = 4
        NUPK = 4 * NSP
        NUNITS = NKV * NUPK
        NPAIR = 16

        def f_sync(eng):
            for kh in range(NKV):
                C.wait(eng, "se_v", NPAIR * NUPK * (kh - 1))
                C.dma(eng, kbuf[:, kh % 2, :], ks_d[kh * 128:(kh + 1) * 128, :], "ld_k3%d" % (kh % 2))
                C.dma_big(eng, vbuf[:, kh % 2, :, :], vs_re[:, :, kh * 128:(kh + 1) * 128], "ld_v3%d" % (kh % 2))
                for qi in range(4):
                    g = kh * 4 + qi
                    C.wait(eng, "se_v", NPAIR * NSP * (g - 1))
                    C.dma(eng, qbuf[:, g % 2, :], qs_d[(kh * 4 + qi) * 128:(kh * 4 + qi + 1) * 128, :], "ld_q3%d" % (g % 2))
        blk.sync(f_sync)

        def spair(eng, p, kb, qb):
            ps = p_sc[p % 3]
            eng.matmul(ps[:, 0, :], kb[:, (2 * p) * 128:(2 * p + 1) * 128], qb, start=True, stop=True)
            inst = eng.matmul(ps[:, 1, :], kb[:, (2 * p + 1) * 128:(2 * p + 2) * 128], qb, start=True, stop=True)
            C.inc(inst, "se_s")

        def f_pe(eng):
            u = 0
            for kh in range(NKV):
                for qi in range(4):
                    g = kh * 4 + qi
                    for sp in range(NSP):
                        base = u * NPAIR
                        if qi == 0 and sp == 0:
                            C.wait(eng, "ld_k3%d" % (kh % 2), 16 * (kh // 2 + 1))
                            C.wait(eng, "ld_v3%d" % (kh % 2), 32 * (kh // 2 + 1))
                        if sp == 0:
                            C.wait(eng, "ld_q3%d" % (g % 2), 16 * (g // 2 + 1))
                        C.wait(eng, "se_pv", base)
                        kb = kbuf[:, kh % 2, :]
                        qb = qbuf[:, g % 2, sp * 512:(sp + 1) * 512]
                        spair(eng, 0, kb, qb)
                        spair(eng, 1, kb, qb)
                        spair(eng, 2, kb, qb)
                        C.wait(eng, "ve_fin", u)
                        for p in range(NPAIR):
                            C.wait(eng, "se_pv", base + p + 1)
                            if p + 3 < NPAIR:
                                spair(eng, p + 3, kb, qb)
                            ptp = pt[:, (2 * p) % 6:(2 * p) % 6 + 2, :]
                            for i in range(2):
                                kt = 2 * p + i
                                inst = eng.matmul(p_pv[:], vbuf[:, kh % 2, kt, :], ptp[:, i, :],
                                                  start=(kt == 0), stop=(kt == 31))
                            C.inc(inst, "se_v")
                        C.wait(eng, "ve_add", 16 * (u + 1))
                        C.inc(eng.matmul(p_mix[0:1, :], ones_colf[:], acc3[:, u % 2, :],
                                         start=True, stop=True), "pe_rs")
                        C.wait(eng, "sc_rec", u + 1)
                        C.inc(eng.matmul(p_mix[:], ones_row[:], rec_row[:], start=True, stop=True), "pe_bc3")
                        u += 1
        blk.tensor(f_pe)

        def f_scalar(eng):
            for u in range(NUNITS):
                h = u // NSP
                base = u * NPAIR
                for p in range(NPAIR):
                    C.wait(eng, "se_s", base + p + 1)
                    C.wait(eng, "se_v", base + p - 2)
                    C.wait(eng, "ve_add", base + p - 2)
                    inst = eng.activation(out=pt[:, (2 * p) % 6:(2 * p) % 6 + 2, :],
                                          in_=p_sc[p % 3][:],
                                          func=AF.Exp, bias=c_all[:, h:h + 1], scale=1.0)
                    C.inc(inst, "se_pv")
                C.wait(eng, "pe_rs", u + 1)
                inst = eng.activation(out=lnrow[:], in_=p_mix[0:1, :], func=AF.Ln)
                C.inc(inst, "sc_ch")
                eng.wait_ge(C.sem("sc_ch"), C.n["sc_ch"])
                C.inc(eng.activation(out=rec_row[:], in_=lnrow[:], func=AF.Exp, scale=-1.0), "sc_rec")
        blk.scalar(f_scalar)

        def f_vec(eng):
            for u in range(NUNITS):
                base = u * NPAIR
                for p in range(NPAIR):
                    C.wait(eng, "se_pv", base + p + 1)
                    ptp = pt[:, (2 * p) % 6:(2 * p) % 6 + 2, :]
                    av = acc3[:, u % 2, :]
                    if p == 0:
                        C.wait(eng, "pe_rs", u - 1)
                        eng.tensor_add(av, ptp[:, 0, :], ptp[:, 1, :])
                    else:
                        eng.tensor_add(av, av, ptp[:, 0, :])
                        eng.tensor_add(av, av, ptp[:, 1, :])
                    C.inc(eng.nop(), "ve_add")
                C.wait(eng, "pe_bc3", u + 1)
                C.vchain(eng, eng.tensor_copy(rec_b[:], p_mix[:]))
                kh = u // NUPK
                C.wait(eng, "st_at%d" % (u % 2), 16 * (u // 2))
                C.inc(eng.scalar_tensor_tensor(atile[:, u % 2, :], p_pv[:], gv_col[:, kh:kh + 1],
                                               rec_b[:], ALU.mult, ALU.mult), "ve_fin")
        blk.vector(f_vec)

        def f_g(eng):
            for u in range(NUNITS):
                h = u // NSP
                sp = u % NSP
                C.wait(eng, "ve_fin", u + 1)
                C.dma(eng, ats_d[h * 128:(h + 1) * 128, sp * 512:(sp + 1) * 512],
                      atile[:, u % 2, :], "st_at%d" % (u % 2))
            C.wait(eng, "st_at0", 16 * NUNITS // 2)
            C.wait(eng, "st_at1", 16 * NUNITS // 2)
        blk.gpsimd(f_g)
        st.close()
    run_phase(p3)

    ln2_stack = ExitStack()
    m2_b = sb("m2_b", [128, T], F32, ln2_stack, side="right")
    r2_b = sb("r2_b", [128, T], F32, ln2_stack, side="right")

    def p4():
        st = ExitStack()
        at_own = sb("at_own", [128, NCH, T], BF16, st)
        wo_ring = sb("wo_ring", [128, 4, 256], F32, st)
        wo_s = sb("wo_s", [128, NCH, 256], BF16, st)
        absb = sb("p4_abs", [128, 2, 256], BF16, st)
        so_fin = sb("so_fin", [128, 2], F32, st)
        xt = sb("xt", [128, 2, 512], F32, st)
        y1t = sb("y1t", [128, 2, 512], F32, st)
        y1bf = sb("y1bf", [128, 2, 512], BF16, st)
        y1sq = sb("y1sq", [128, 2, 512], BF16, st)
        acc_s = sb("acc_s", [1, T], F32, st)
        acc_q = sb("acc_q", [1, T], F32, st)
        srow = sb("srow", [1, T], F32, st)
        p_so = [psum("p_so_%d" % j, [128, 1], F32, st) for j in range(2)]
        p_o = [psum("p_o%d" % i, [128, 512], F32, st) for i in range(2)]
        p_s4 = psum("p_s4", [1, 512], F32, st)
        p_q4 = psum("p_q4", [1, 512], F32, st)
        ats_re = ats_d.rearrange("(c p) t -> p c t", p=128)

        def f_sync(eng):
            for i in range(4):
                C.dma_big(eng, at_own[:, :, i * 512:(i + 1) * 512],
                          ats_re[:, :, i * 512:(i + 1) * 512], "ld_at")
            for hs in range(16):
                for c in range(NCH):
                    k = hs * NCH + c
                    C.wait(eng, "sc_o", k - 3)
                    C.wait(eng, "ve_oab", k - 3)
                    C.dma(eng, wo_ring[:, k % 4, :],
                          wo_t[c * 128:(c + 1) * 128, hs * 256:(hs + 1) * 256], "ld_wo%d" % (k % 4))
                for j in range(2):
                    for ts_ in range(4):
                        u = hs * 8 + j * 4 + ts_
                        ot = hs * 2 + j
                        C.wait(eng, "ve_y1", u - 1)
                        C.dma(eng, xt[:, u % 2, :],
                              x_all[ot * 128:(ot + 1) * 128, ts_ * 512:(ts_ + 1) * 512], "ld_xt%d" % (u % 2))
        blk.sync(f_sync)

        def f_scalar(eng):
            for hs in range(16):
                for c in range(NCH):
                    k = hs * NCH + c
                    C.wait(eng, "ld_wo%d" % (k % 4), 16 * (k // 4 + 1))
                    if c == 0:
                        C.wait(eng, "pe_o", hs * 8)
                    C.inc(eng.activation(out=wo_s[:, c, :], in_=wo_ring[:, k % 4, :], func=AF.Sign), "sc_o")
                    C.wait(eng, "pe_so", k - 1)
                    C.inc(eng.activation(out=absb[:, k % 2, :], in_=wo_ring[:, k % 4, :], func=AF.Abs), "ve_oab")
                for ui in range(8):
                    u = hs * 8 + ui
                    C.wait(eng, "ve_y1", u + 1)
                    C.wait(eng, "pe_s4", u - 1)
                    eng.activation(out=y1bf[:, u % 2, :], in_=y1t[:, u % 2, :], func=AF.Copy)
                    C.inc(eng.activation(out=y1sq[:, u % 2, :], in_=y1t[:, u % 2, :], func=AF.Square), "sc_y1")
        blk.scalar(f_scalar)

        def stats_mm(eng, u):
            C.wait(eng, "sc_y1", u + 1)
            C.wait(eng, "ve_s4", u)
            eng.matmul(p_s4[:], ones_col[:], y1bf[:, u % 2, :], start=True, stop=True)
            inst = eng.matmul(p_q4[:], ones_col[:], y1sq[:, u % 2, :], start=True, stop=True)
            C.inc(inst, "pe_s4")

        def f_pe(eng):
            for hs in range(16):
                for c in range(NCH):
                    k = hs * NCH + c
                    C.wait(eng, "ve_oab", k + 1)
                    for j in range(2):
                        inst = eng.matmul(p_so[j][:], absb[:, k % 2, j * 128:(j + 1) * 128],
                                          ones_col[:], start=(c == 0), stop=(c == NCH - 1))
                    C.inc(inst, "pe_so")
                C.wait(eng, "sc_o", (hs + 1) * NCH)
                C.wait(eng, "ve_go", hs + 1)
                if hs == 0:
                    C.wait(eng, "ld_at", 128)
                for j in range(2):
                    for ts_ in range(4):
                        u = hs * 8 + j * 4 + ts_
                        C.wait(eng, "ve_y1", u - 1)
                        for c in range(NCH):
                            inst = eng.matmul(p_o[u % 2][:], wo_s[:, c, j * 128:(j + 1) * 128],
                                              at_own[:, c, ts_ * 512:(ts_ + 1) * 512],
                                              start=(c == 0), stop=(c == NCH - 1))
                        C.inc(inst, "pe_o")
                        if u >= 1:
                            stats_mm(eng, u - 1)
            stats_mm(eng, 127)
        blk.tensor(f_pe)

        def f_vec(eng):
            eng.memset(acc_s[:], 0.0)
            eng.memset(acc_q[:], 0.0)
            for hs in range(16):
                C.wait(eng, "pe_so", (hs + 1) * NCH)
                for j in range(2):
                    inst = eng.tensor_scalar(so_fin[:, j:j + 1], p_so[j][:], 1.0 / D, 1e-8, ALU.mult, ALU.max)
                C.vchain(eng, inst)
                for j in range(2):
                    ot = hs * 2 + j
                    inst = eng.tensor_mul(go_col[:, ot:ot + 1], so_fin[:, j:j + 1], sc_os[:, ot:ot + 1])
                C.inc(inst, "ve_go")
                C.waitn(eng, "ve_go")
                for j in range(2):
                    for ts_ in range(4):
                        u = hs * 8 + j * 4 + ts_
                        ot = hs * 2 + j
                        C.wait(eng, "pe_o", u + 1)
                        C.wait(eng, "ld_xt%d" % (u % 2), 16 * (u // 2 + 1))
                        C.wait(eng, "st_y1%d" % (u % 2), 16 * (u // 2))
                        C.wait(eng, "sc_y1", u - 1)
                        C.inc(eng.scalar_tensor_tensor(y1t[:, u % 2, :], p_o[u % 2][:],
                                                       go_col[:, ot:ot + 1], xt[:, u % 2, :],
                                                       ALU.mult, ALU.add), "ve_y1")
                        C.wait(eng, "pe_s4", u)
                        if u >= 1:
                            up = u - 1
                            tsp = up % 4
                            eng.tensor_add(acc_s[:, tsp * 512:(tsp + 1) * 512],
                                           acc_s[:, tsp * 512:(tsp + 1) * 512], p_s4[:])
                            C.inc(eng.tensor_add(acc_q[:, tsp * 512:(tsp + 1) * 512],
                                                 acc_q[:, tsp * 512:(tsp + 1) * 512], p_q4[:]), "ve_s4")
            C.wait(eng, "pe_s4", 128)
            tsp = 127 % 4
            eng.tensor_add(acc_s[:, tsp * 512:(tsp + 1) * 512],
                           acc_s[:, tsp * 512:(tsp + 1) * 512], p_s4[:])
            C.inc(eng.tensor_add(acc_q[:, tsp * 512:(tsp + 1) * 512],
                                 acc_q[:, tsp * 512:(tsp + 1) * 512], p_q4[:]), "ve_s4")
            C.vchain(eng, eng.tensor_scalar_mul(acc_s[:], acc_s[:], 1.0 / D))
            C.vchain(eng, eng.tensor_scalar_mul(acc_q[:], acc_q[:], 1.0 / D))
            C.vchain(eng, eng.tensor_mul(srow[:], acc_s[:], acc_s[:]))
            C.inc(eng.tensor_sub(acc_q[:], acc_q[:], srow[:]), "ve_v2")
        blk.vector(f_vec)

        def f_g(eng):
            for u in range(128):
                ot = (u // 8) * 2 + (u % 8) // 4
                ts_ = u % 4
                C.wait(eng, "ve_y1", u + 1)
                C.dma(eng, y1s_d[ot * 128:(ot + 1) * 128, ts_ * 512:(ts_ + 1) * 512],
                      y1t[:, u % 2, :], "st_y1%d" % (u % 2))
            C.wait(eng, "st_y10", 16 * 64)
            C.wait(eng, "st_y11", 16 * 64)
        blk.gpsimd(f_g)

        def f_sfin(eng):
            C.wait(eng, "ve_v2", 1)
            C.inc(eng.activation(out=srow[:], in_=acc_q[:], func=AF.Sqrt, bias=eps1[:]), "sc_sd2")
        blk.scalar(f_sfin)

        def f_vfin2(eng):
            C.wait(eng, "sc_sd2", 1)
            C.inc(eng.reciprocal(srow[:], srow[:]), "ve_r2")
        blk.vector(f_vfin2)

        def f_pfin(eng):
            C.wait(eng, "ve_r2", 1)
            for i in range(4):
                C.wait(eng, "ve_bc2", i)
                eng.matmul(p_o[0][:], ones_row[:], acc_s[:, i * 512:(i + 1) * 512], start=True, stop=True)
                inst = eng.matmul(p_o[1][:], ones_row[:], srow[:, i * 512:(i + 1) * 512], start=True, stop=True)
                C.inc(inst, "pe_bc2")
        blk.tensor(f_pfin)

        def f_vfin3(eng):
            for i in range(4):
                C.wait(eng, "pe_bc2", i + 1)
                eng.tensor_copy(m2_b[:, i * 512:(i + 1) * 512], p_o[0][:])
                C.inc(eng.tensor_copy(r2_b[:, i * 512:(i + 1) * 512], p_o[1][:]), "ve_bc2")
        blk.vector(f_vfin3)
        st.close()
    run_phase(p4)

    p67_stack = ExitStack()
    xn2 = sb("xn2", [128, NCH, T], BF16, p67_stack)
    hw_bf = sb("hw_bf", [128, T], BF16, p67_stack)
    a_bf = sb("a_bf", [128, NCH, 128], BF16, p67_stack)
    b_bf = sb("b_bf", [128, D], BF16, p67_stack)
    wg_bf = sb("wg_bf", [128, NCH, 8], BF16, p67_stack)

    def p5():
        st = ExitStack()
        y1c = sb("y1c", [128, 2, T], F32, st)
        tmp5 = sb("tmp5", [128, T], F32, st)

        def f_g(eng):
            for c in range(NCH):
                C.wait(eng, "ve_x2", c - 1)
                C.dma(eng, y1c[:, c % 2, :], y1s_d[c * 128:(c + 1) * 128, :], "ld_y1%d" % (c % 2))
        blk.gpsimd(f_g)

        def f_vec(eng):
            for c in range(NCH):
                C.wait(eng, "ld_y1%d" % (c % 2), 16 * (c // 2 + 1))
                C.vchain(eng, eng.tensor_sub(tmp5[:], y1c[:, c % 2, :], m2_b[:]))
                C.inc(eng.tensor_mul(xn2[:, c, :], tmp5[:], r2_b[:]), "ve_x2")
        blk.vector(f_vec)
        st.close()
    run_phase(p5)
    ln2_stack.close()

    def p6():
        st = ExitStack()
        ab32 = sb("ab32", [128, 2048], F32, st)
        lg = sb("lg", [128, 8], F32, st)
        nm1 = sb("nm1", [128, 1], F32, st)
        m1 = sb("m1", [128, 1], F32, st)
        m2r = sb("m2r", [128, 1], F32, st)
        mask1 = sb("mask1", [128, 8], F32, st)
        lmask = sb("lmask", [128, 8], F32, st)
        expt = sb("expt", [128, 8], F32, st)
        rwun = sb("rwun", [128, 8], F32, st)
        den = sb("den", [128, 1], F32, st)
        den2 = sb("den2", [128, 1], F32, st)
        rw_bf = sb("rw_bf", [128, 8], BF16, st)
        rwt = sb("rwt", [8, 128], BF16, st)
        re_sb = sb("re_sb", [128, T], BF16, st)
        p_lg = psum("p_lg", [128, 8], F32, st)
        p_rt = psum("p_rt", [8, 128], BF16, st)
        p_ex = psum("p_ex", [128, 128], F32, st)
        p_h = [psum("p_h%d" % i, [128, 512], F32, st) for i in range(2)]

        def f_sync(eng):
            C.dma(eng, ab32[:, 0:256], wg_col, "ld_p6")
            for hf in range(2):
                C.wait(eng, "sc_p6", hf + 1)
                C.dma(eng, ab32[:, 0:2048], a_col[:, hf * 2048:(hf + 1) * 2048], "ld_p6")
            for hf in range(2):
                C.wait(eng, "sc_p6", hf + 3)
                C.dma(eng, ab32[:, 0:2048], b_re[:, hf * 2048:(hf + 1) * 2048], "ld_p6")
        blk.sync(f_sync)

        def f_scalar(eng):
            C.wait(eng, "ld_p6", 16)
            C.inc(eng.activation(out=wg_bf[:], in_=ab32[:, 0:256].rearrange("p (c e) -> p c e", e=8),
                                 func=AF.Copy), "sc_p6")
            for hf in range(2):
                C.wait(eng, "ld_p6", 16 * (hf + 2))
                C.inc(eng.activation(out=a_bf[:, hf * 16:(hf + 1) * 16, :],
                                     in_=ab32[:, 0:2048].rearrange("p (c d) -> p c d", d=128),
                                     func=AF.Copy), "sc_p6")
            for hf in range(2):
                C.wait(eng, "ld_p6", 16 * (hf + 4))
                C.inc(eng.activation(out=b_bf[:, hf * 2048:(hf + 1) * 2048],
                                     in_=ab32[:, 0:2048], func=AF.Copy), "sc_p6")
            for tt in range(16):
                C.wait(eng, "ve_rt", tt * 10 + 4)
                C.inc(eng.activation(out=expt[:], in_=lg[:], func=AF.Exp, bias=nm1[:]), "sc_rt")
        blk.scalar(f_scalar)

        def f_pe(eng):
            C.wait(eng, "sc_p6", 1)
            for tt in range(16):
                C.wait(eng, "ve_rt", tt * 10)
                C.wait(eng, "ve_re", tt)
                for c in range(NCH):
                    inst = eng.matmul(p_lg[:], xn2[:, c, tt * 128:(tt + 1) * 128], wg_bf[:, c, :],
                                      start=(c == 0), stop=(c == NCH - 1))
                C.inc(inst, "pe_lg")
                C.wait(eng, "ve_rt", tt * 10 + 9)
                C.inc(eng.transpose(p_rt[:], rw_bf[:], ident_bf[:]), "pe_lg")
                C.wait(eng, "ve_rt", tt * 10 + 10)
                C.inc(eng.matmul(p_ex[:], mexp_bf[:], rwt[:], start=True, stop=True), "pe_lg")
            C.wait(eng, "sc_p6", 3)
            C.wait(eng, "ve_re", 16)
            for ts_ in range(4):
                C.wait(eng, "ve_hw", ts_ - 1)
                for c in range(NCH):
                    inst = eng.matmul(p_h[ts_ % 2][:], a_bf[:, c, :], xn2[:, c, ts_ * 512:(ts_ + 1) * 512],
                                      start=(c == 0), stop=(c == NCH - 1))
                C.inc(inst, "pe_h")
        blk.tensor(f_pe)

        def f_vec(eng):
            for tt in range(16):
                C.wait(eng, "pe_lg", tt * 3 + 1)
                i1 = eng.tensor_copy(lg[:], p_lg[:]); C.inc(i1, "ve_rt"); C.waitn(eng, "ve_rt")
                i1 = eng.reduce_max(nm1[:], lg[:], axis=AX.X, negate=True); C.inc(i1, "ve_rt"); C.waitn(eng, "ve_rt")
                i1 = eng.tensor_scalar_mul(m1[:], nm1[:], -1.0); C.inc(i1, "ve_rt"); C.waitn(eng, "ve_rt")
                i1 = eng.tensor_scalar(mask1[:], lg[:], m1[:], None, ALU.is_ge); C.inc(i1, "ve_rt"); C.waitn(eng, "ve_rt")
                i1 = eng.scalar_tensor_tensor(lmask[:], mask1[:], -1e30, lg[:], ALU.mult, ALU.add); C.inc(i1, "ve_rt"); C.waitn(eng, "ve_rt")
                i1 = eng.reduce_max(m2r[:], lmask[:], axis=AX.X); C.inc(i1, "ve_rt"); C.waitn(eng, "ve_rt")
                i1 = eng.tensor_scalar(rwun[:], lg[:], m2r[:], None, ALU.is_ge); C.inc(i1, "ve_rt"); C.waitn(eng, "ve_rt")
                C.inc(eng.nop(), "ve_rt")
                C.wait(eng, "sc_rt", tt + 1)
                C.vchain(eng, eng.tensor_mul(rwun[:], rwun[:], expt[:]))
                C.vchain(eng, eng.reduce_sum(den[:], rwun[:], axis=AX.X))
                C.vchain(eng, eng.reciprocal(den2[:], den[:]))
                C.vchain(eng, eng.tensor_scalar_mul(rwun[:], rwun[:], den2[:]))
                C.inc(eng.tensor_copy(rw_bf[:], rwun[:]), "ve_rt")
                C.wait(eng, "pe_lg", tt * 3 + 2)
                C.inc(eng.tensor_copy(rwt[:], p_rt[:]), "ve_rt")
                C.wait(eng, "pe_lg", tt * 3 + 3)
                C.inc(eng.tensor_copy(re_sb[:, tt * 128:(tt + 1) * 128], p_ex[:]), "ve_re")
            for ts_ in range(4):
                C.wait(eng, "pe_h", ts_ + 1)
                C.inc(eng.tensor_mul(hw_bf[:, ts_ * 512:(ts_ + 1) * 512], p_h[ts_ % 2][:],
                                     re_sb[:, ts_ * 512:(ts_ + 1) * 512]), "ve_hw")
        blk.vector(f_vec)
        st.close()
    run_phase(p6)
    if dbg:
        def f_dbg2(eng):
            C.dma(eng, dbg_go, go_col[:], "st_dbg")
            C.dma(eng, dbg_hw, hw_bf[:], "st_dbg")
            for c2 in range(NCH):
                C.dma(eng, dbg_xn2.rearrange("(c p) t -> p c t", p=128)[:, c2, :], xn2[:, c2, :], "st_dbg")
            eng.wait_ge(C.sem("st_dbg"), C.n["st_dbg"])
        blk.gpsimd(f_dbg2)
        barrier()

    def p7():
        st = ExitStack()
        wb_ring = sb("wb_ring", [128, 4, 256], F32, st)
        wb_s = sb("wb_s", [128, NCH, 256], BF16, st)
        y1tile = sb("y1tile", [128, 2, 512], F32, st)
        y2t = sb("y2t", [128, 2, 512], F32, st)
        ot_t = sb("ot_t", [128, 2, 128], F32, st)
        p_f = [psum("p_f%d" % i, [128, 512], F32, st) for i in range(2)]
        p_tr = [psum("p_tr%d" % i, [128, 128], F32, st) for i in range(2)]

        def f_sync(eng):
            for hs in range(16):
                for c in range(NCH):
                    k = hs * NCH + c
                    C.wait(eng, "sc_b", k - 3)
                    C.dma(eng, wb_ring[:, k % 4, :],
                          wb_t[c * 128:(c + 1) * 128, hs * 256:(hs + 1) * 256], "ld_wb%d" % (k % 4))
                for j in range(2):
                    for ts_ in range(4):
                        u = hs * 8 + j * 4 + ts_
                        ot = hs * 2 + j
                        C.wait(eng, "ve_y2", u - 1)
                        C.dma(eng, y1tile[:, u % 2, :],
                              y1s_d[ot * 128:(ot + 1) * 128, ts_ * 512:(ts_ + 1) * 512], "ld_y1t%d" % (u % 2))
        blk.sync(f_sync)

        def f_scalar(eng):
            for hs in range(16):
                for c in range(NCH):
                    k = hs * NCH + c
                    C.wait(eng, "ld_wb%d" % (k % 4), 16 * (k // 4 + 1))
                    if c == 0:
                        C.wait(eng, "pe_f", hs * 8)
                    C.inc(eng.activation(out=wb_s[:, c, :], in_=wb_ring[:, k % 4, :], func=AF.Copy), "sc_b")
        blk.scalar(f_scalar)

        def trans_mm(eng, u):
            for sseg in range(4):
                C.wait(eng, "ve_ot", u * 4 + sseg - 1)
                C.inc(eng.transpose(p_tr[sseg % 2][:], y2t[:, u % 2, sseg * 128:(sseg + 1) * 128],
                                    ident[:]), "pe_tr")

        def f_pe(eng):
            for hs in range(16):
                C.wait(eng, "sc_b", (hs + 1) * NCH)
                for j in range(2):
                    ot = hs * 2 + j
                    for ts_ in range(4):
                        u = hs * 8 + j * 4 + ts_
                        C.wait(eng, "ve_y2", u - 1)
                        for c in range(NCH):
                            eng.matmul(p_f[u % 2][:], wb_s[:, c, j * 128:(j + 1) * 128],
                                       xn2[:, c, ts_ * 512:(ts_ + 1) * 512],
                                       start=(c == 0), stop=False)
                        inst = eng.matmul(p_f[u % 2][:], b_bf[:, ot * 128:(ot + 1) * 128],
                                          hw_bf[:, ts_ * 512:(ts_ + 1) * 512], start=False, stop=True)
                        C.inc(inst, "pe_f")
                        if u >= 1:
                            C.wait(eng, "ve_y2", u)
                            trans_mm(eng, u - 1)
            C.wait(eng, "ve_y2", 128)
            trans_mm(eng, 127)
        blk.tensor(f_pe)

        def f_vec(eng):
            for hs in range(16):
                for j in range(2):
                    ot = hs * 2 + j
                    for ts_ in range(4):
                        u = hs * 8 + j * 4 + ts_
                        C.wait(eng, "pe_f", u + 1)
                        C.wait(eng, "ld_y1t%d" % (u % 2), 16 * (u // 2 + 1))
                        C.wait(eng, "pe_tr", 4 * (u - 1))
                        C.inc(eng.scalar_tensor_tensor(y2t[:, u % 2, :], p_f[u % 2][:],
                                                       sc_bb[:, ot:ot + 1], y1tile[:, u % 2, :],
                                                       ALU.add, ALU.add), "ve_y2")
                        if u >= 1:
                            up = u - 1
                            for sseg in range(4):
                                idx = up * 4 + sseg
                                C.wait(eng, "pe_tr", idx + 1)
                                C.wait(eng, "st_out%d" % (idx % 2), 16 * (idx // 2))
                                C.inc(eng.tensor_copy(ot_t[:, sseg % 2, :], p_tr[sseg % 2][:]), "ve_ot")
            up = 127
            for sseg in range(4):
                idx = up * 4 + sseg
                C.wait(eng, "pe_tr", idx + 1)
                C.wait(eng, "st_out%d" % (idx % 2), 16 * (idx // 2))
                C.inc(eng.tensor_copy(ot_t[:, sseg % 2, :], p_tr[sseg % 2][:]), "ve_ot")
        blk.vector(f_vec)

        def f_g(eng):
            for u in range(128):
                hs = u // 8
                j = (u % 8) // 4
                ts_ = u % 4
                ot = hs * 2 + j
                for sseg in range(4):
                    C.wait(eng, "ve_ot", u * 4 + sseg + 1)
                    row = ts_ * 512 + sseg * 128
                    C.dma(eng, out[row:row + 128, ot * 128:(ot + 1) * 128],
                          ot_t[:, sseg % 2, :], "st_out%d" % ((u * 4 + sseg) % 2))
            C.wait(eng, "st_out0", 16 * 256)
            C.wait(eng, "st_out1", 16 * 256)
        blk.gpsimd(f_g)
        st.close()
    run_phase(p7)

    blk.__exit__(None, None, None)
    p67_stack.close()
    es.close()
    return nc


def kernel(L0_attn_hidden_states, L0_attn_q_weight, L0_attn_q_scale,
           L0_attn_k_weight, L0_attn_k_scale, L0_attn_v_weight, L0_attn_v_scale,
           L0_attn_o_weight, L0_attn_o_scale, L0_ffn_W_base, L0_ffn_b_base,
           L0_ffn_W_gate, L0_ffn_A, L0_ffn_B):
    x = np.asarray(L0_attn_hidden_states, dtype=np.float32)
    wq = np.asarray(L0_attn_q_weight, np.float32)
    wk = np.asarray(L0_attn_k_weight, np.float32)
    wv = np.asarray(L0_attn_v_weight, np.float32)
    wo = np.asarray(L0_attn_o_weight, np.float32)
    wb = np.asarray(L0_ffn_W_base, np.float32)
    wg = np.asarray(L0_ffn_W_gate, np.float32)
    A = np.asarray(L0_ffn_A, np.float32)
    B = np.asarray(L0_ffn_B, np.float32)

    shared = dict(
        wq_t=np.ascontiguousarray(wq.T),
        wk_t=np.ascontiguousarray(wk.T),
        wv_t=np.ascontiguousarray(wv.T),
        wo_t=np.ascontiguousarray(wo.T),
        wb_t=np.ascontiguousarray(wb.T),
        wg_col=np.ascontiguousarray(wg.T.reshape(32, 128, 8).transpose(1, 0, 2).reshape(128, 256)),
        a_col=np.ascontiguousarray(A.transpose(2, 0, 1).reshape(4096, 128).reshape(32, 128, 128).transpose(1, 0, 2).reshape(128, 4096)),
        b_re=np.ascontiguousarray(B.transpose(0, 2, 1).reshape(128, 4096)),
        qs_col=np.ascontiguousarray(np.asarray(L0_attn_q_scale, np.float32).reshape(32, 128).T),
        ks_col=np.ascontiguousarray(np.asarray(L0_attn_k_scale, np.float32).reshape(8, 128).T),
        vs_col=np.ascontiguousarray(np.asarray(L0_attn_v_scale, np.float32).reshape(8, 128).T),
        os_col=np.ascontiguousarray(np.asarray(L0_attn_o_scale, np.float32).reshape(32, 128).T),
        bb_col=np.ascontiguousarray(np.asarray(L0_ffn_b_base, np.float32).reshape(32, 128).T),
        ident_in=np.eye(128, dtype=np.float32),
        mexp_in=np.kron(np.eye(8), np.ones((1, 16))).astype(np.float32).reshape(8, 128),
    )

    xt = np.ascontiguousarray(x.transpose(0, 2, 1))
    in_maps = []
    for core in range(8):
        b, h = core // 2, core % 2
        own = xt[b][:, h * T:(h + 1) * T]
        oth = xt[b][:, (1 - h) * T:(2 - h) * T]
        m = dict(shared)
        m["x_all"] = np.ascontiguousarray(np.concatenate([own, oth], axis=1))
        in_maps.append(m)

    nc = build_nc()
    res = run_bass_kernel_spmd(nc, in_maps, core_ids=list(range(8)))

    outp = np.empty((4, 4096, 4096), np.float32)
    for core in range(8):
        b, h = core // 2, core % 2
        outp[b, h * T:(h + 1) * T, :] = res.results[core]["out"]
    return outp
